# version 04
# baseline (speedup 1.0000x reference)
import os
import numpy as np
import concourse.bass as bass
import concourse.mybir as mybir
from concourse.bass_utils import run_bass_kernel_spmd
from concourse.alu_op_type import AluOpType as OP

F32 = mybir.dt.float32
BF16 = mybir.dt.bfloat16
AF = mybir.ActivationFunctionType
AX = mybir.AxisListType.X

S = 2048
DM = 1024
NT = S // 128
NCH = DM // 128
N_SUB = 6
L1CUT = int(os.environ.get("L1CUT", "9"))


class Cmd:
    __slots__ = ("fn", "waits", "skey", "delta", "inc_val")
    def __init__(self, fn, waits, skey, delta):
        self.fn = fn; self.waits = waits; self.skey = skey; self.delta = delta
        self.inc_val = None


class Ctx:
    ENGS = ("sq", "pool", "pe", "dve", "act")

    def __init__(self, nc):
        self.nc = nc
        self.cmds = {e: [] for e in self.ENGS}
        self.attached = {e: [] for e in self.ENGS}
        self.cum = {}
        self.known = {}
        self.bufs = {}
        self.pending = {e: {} for e in self.ENGS}

    def _event(self, eng, idx):
        arr = self.attached[eng]
        lo, hi = 0, len(arr)
        while lo < hi:
            mid = (lo + hi) // 2
            if arr[mid][0] < idx: lo = mid + 1
            else: hi = mid
        if lo < len(arr):
            return arr[lo][1], arr[lo][2]
        c = self.cmds[eng][idx]
        v = self.cum.get(c.skey, 0) + c.delta
        self.cum[c.skey] = v
        c.inc_val = v
        arr.append((idx, c.skey, v))
        return c.skey, v

    def op(self, eng, fn, reads=(), writes=(), dma=False):
        deps = {}
        def dep(se, si):
            if se == eng == "pe":
                return
            key, v = self._event(se, si)
            if self.known.get((eng, key), 0) >= v: return
            if deps.get(key, 0) < v: deps[key] = v
        for b in reads:
            st = self.bufs.get(b)
            if st and st["w"]: dep(*st["w"])
        for b in writes:
            st = self.bufs.get(b)
            if st:
                if st["w"]: dep(*st["w"])
                for se, si in st["r"].items(): dep(se, si)
        for key, v in self.pending[eng].items():
            if self.known.get((eng, key), 0) < v and deps.get(key, 0) < v: deps[key] = v
        self.pending[eng] = {}
        idx = len(self.cmds[eng])
        for b in writes:
            self.bufs[b] = {"w": (eng, idx), "r": {}}
        for b in reads:
            st = self.bufs.setdefault(b, {"w": None, "r": {}})
            st["r"][eng] = idx
        for key, v in deps.items():
            self.known[(eng, key)] = v
        if dma:
            skey, delta = eng + "_d_" + (dma if isinstance(dma, str) else "x"), 16
        else:
            skey, delta = eng, 1
        self.cmds[eng].append(Cmd(fn, sorted(deps.items()), skey, delta))
        if dma:
            self._event(eng, idx)
        return (eng, idx)

    def dma(self, q, dst, src, reads=(), writes=(), ch=None):
        c = ch or (writes[0] if writes else "x")
        wl = list(writes) + ["__ch_%s_%s" % (q, c)]
        if q == "pool":
            n = self.cum.get("__pooldma_n", 0)
            self.cum["__pooldma_n"] = n + 1
            wl.append("__pool_rr%d" % (n % 2))
        return self.op(q, lambda eng, d=dst, s=src: eng.dma_start(d, s),
                       reads=reads, writes=wl, dma=c)

    def barrier(self):
        evs = []
        for e in self.ENGS:
            n = len(self.cmds[e])
            if n: evs.append((e, self._event(e, n - 1)))
        for e in self.ENGS:
            for se, (key, v) in evs:
                if se == e == "pe": continue
                if self.known.get((e, key), 0) < v:
                    if self.pending[e].get(key, 0) < v: self.pending[e][key] = v

    def finalize(self, sems):
        with self.nc.Block() as block:
            def mk(ename):
                def body(eng):
                    for c in self.cmds[ename]:
                        for key, v in c.waits:
                            eng.wait_ge(sems[key], v)
                        inst = c.fn(eng)
                        if c.inc_val is not None:
                            inst.then_inc(sems[c.skey], c.delta)
                return body
            block.sync(mk("sq"))
            block.gpsimd(mk("pool"))
            block.tensor(mk("pe"))
            block.vector(mk("dve"))
            block.scalar(mk("act"))



def build_program(nsub=N_SUB):
    nc = bass.Bass("TRN2", target_bir_lowering=False)
    ctx = Ctx(nc)
    o, dma = ctx.op, ctx.dma

    def din(name, shape):
        return nc.dram_tensor(name, shape, F32, kind="ExternalInput").ap()

    x_in = din("x", [S, DM])
    y_out = nc.dram_tensor("y", [S, DM], F32, kind="ExternalOutput").ap()
    W = {}
    for name, shape in [
        ("L0_attn_Wq", [DM, DM]), ("L0_attn_Wkv_short", [DM, 2 * DM]),
        ("L0_attn_Wkv_long", [DM, 2 * DM]), ("L0_attn_Wo", [DM, DM]),
        ("L0_attn_mix_weight", [1, 1]), ("L0_attn_decay_factor", [1, 1]),
        ("L0_ffn_Wr", [DM, 8]), ("L0_ffn_W1", [8, DM, 2048]),
        ("L0_ffn_W2", [8, 2048, DM]), ("L0_ffn_W3", [8, DM, 2048]),
        ("L1_attn_Wqkv", [DM, 3 * DM]), ("L1_attn_bqkv", [3 * DM]),
        ("L1_attn_Wg", [DM, 2730]), ("L1_attn_bg", [2730]),
        ("L1_attn_Wu", [DM, 2730]), ("L1_attn_bu", [2730]),
        ("L1_attn_Wd", [2730, DM]), ("L1_attn_bd", [DM]),
        ("L1_attn_g1", [DM]), ("L1_attn_b1", [DM]),
        ("L1_attn_g2", [DM]), ("L1_attn_b2", [DM]),
        ("L1_ffn_Wr1", [DM, 256]), ("L1_ffn_br1", [256]),
        ("L1_ffn_Wr2", [256, 8]), ("L1_ffn_br2", [8]),
        ("L1_ffn_Ws1", [DM, 4096]), ("L1_ffn_bs1", [4096]),
        ("L1_ffn_Ws2", [4096, DM]), ("L1_ffn_bs2", [DM]),
        ("L1_ffn_We1", [8, DM, 4096]), ("L1_ffn_be1", [8, 4096]),
        ("L1_ffn_We2", [8, 4096, DM]), ("L1_ffn_be2", [8, DM]),
    ]:
        W[name] = din(name, shape)

    xs = nc.dram_tensor("xscr", [128, NT * DM], F32).ap()
    ktl_s = nc.dram_tensor("ktlscr", [128, NCH * S], BF16).ap()
    vl_s = nc.dram_tensor("vlscr", [128, NT * 1040], BF16).ap()

    cms = []
    def sb(name, shape, dt):
        cm = nc.sbuf_tensor(name, shape, dt); t = cm.__enter__(); cms.append(cm); return t

    A2C = 43008
    XA = sb("XA", [128, NT * DM], F32)
    A2 = sb("A2", [128, A2C], BF16)
    A3 = sb("A3", [128, NT * 1040], BF16)
    TMP = sb("TMP", [128, 3840], F32)
    IDB = sb("IDB", [128, 128], BF16)
    IDF = sb("IDF", [128, 128], F32)
    IOP = sb("IOP", [128, 1], F32)
    IOF = sb("IOF", [128, 1024], F32)
    ONEF = sb("ONEF", [1, 128], F32)
    BCOL = sb("BCOL", [128, 160], F32)
    BMAT = sb("BMAT", [16, 1024], F32)
    GAT = sb("GAT", [128, NT * 10], F32)
    SMALL = sb("SMALL", [128, 64], F32)
    EMSK = sb("EMSK", [128, 128], BF16)
    PScm = nc.psum_tensor("PS", [128, 4096], F32)
    PSh = PScm.__enter__(); cms.append(PScm)

    sems = {}

    xa, a2, a3, tmp, ps = XA.ap(), A2.ap(), A3.ap(), TMP.ap(), PSh.ap()
    xab = xa.bitcast(BF16)

    def psv(bank, cols=512):
        return ps[:, bank * 512:bank * 512 + cols]
    def pk(bank, cols=512):
        return ["ps%d" % b for b in range(bank, bank + (cols + 511) // 512)]

    XKEYS = ["xa%d" % t for t in range(NT)]

    def ldw_bf(dst_off, ncols_chunks, chunk_cols, src_ap, key):
        dst = a2[:, dst_off:dst_off + ncols_chunks * chunk_cols].rearrange(
            "p (c m) -> p c m", c=ncols_chunks)
        dma("pool", dst, src_ap, writes=[key])

    def w2d(wap, r0, r1, c0, c1):
        return wap[r0:r1, c0:c1].rearrange("(c p) m -> p c m", p=128)

    def w3d(wap, e_i, r0, r1, c0, c1):
        return wap[e_i:e_i + 1, r0:r1, c0:c1].rearrange("a (c p) m -> p (a c) m", p=128)

    def bias_cols(src_row_ap, n, col0, keyr, keyc, zero_tail=False):
        m = (n + 127) // 128
        rows = TMP[0:m, 0:128]
        if zero_tail:
            o("pool", lambda e: e.memset(TMP[max(0, m - 1):m, 0:128], 0.0), writes=[keyr])
        full = (n // 128) * 128
        if full:
            dma("sq", TMP[0:n // 128, 0:128],
                src_row_ap[0:full].rearrange("(r p) -> r p", p=128),
                writes=[keyr])
        if n % 128:
            dma("sq", TMP[n // 128:n // 128 + 1, 0:n % 128], src_row_ap[None, full:n],
                writes=[keyr])
        pt = psv(7, 128)
        o("pe", lambda e: e.matmul(pt[:, 0:m], rows, IDF[0:m, 0:m], start=True, stop=True),
          reads=[keyr, "idf"], writes=pk(7))
        o("dve", lambda e: e.tensor_copy(BCOL[:, col0:col0 + m], pt[:, 0:m]),
          reads=pk(7), writes=[keyc])

    def transpose_128(src_ap, dst_base_off, dst_stride, n_sub, rkeys, wkey, dst_sub0=0):
        for g0 in range(0, n_sub, 4):
            gn = min(4, n_sub - g0)
            pb = 6 + (g0 // 4) % 2
            pview = psv(pb, 512)
            for j in range(gn):
                o("pe", lambda e, j=j, g0=g0, pview=pview: e.matmul(
                    pview[:, j * 128:(j + 1) * 128],
                    src_ap[:, (g0 + j) * 128:(g0 + j + 1) * 128], IDB[:, :],
                    start=(j == 0), stop=(j == gn - 1), skip_group_check=True),
                  reads=rkeys + ["idb"], writes=pk(pb))
            def evac(e, g0=g0, gn=gn, pview=pview):
                d = a2[:, dst_base_off + (dst_sub0 + g0) * dst_stride:
                       dst_base_off + (dst_sub0 + g0 + (gn - 1)) * dst_stride + 128]
                d = bass.AP(A2, dst_base_off + (dst_sub0 + g0) * dst_stride,
                            [[A2C, 128], [dst_stride, gn], [1, 128]])
                return e.tensor_copy(d, pview[:, 0:gn * 128])
            o("dve", evac, reads=pk(pb), writes=[wkey])

    def setup():
        o("pool", lambda e: e.iota(IOP[:, :], [[0, 1]], channel_multiplier=1,
                                   allow_small_or_imprecise_dtypes=True), writes=["iop"])
        o("pool", lambda e: e.iota(IOF[:, :], [[1, 1024]], channel_multiplier=0,
                                   allow_small_or_imprecise_dtypes=True), writes=["iof"])
        o("pool", lambda e: e.iota(TMP[:, 0:128], [[1, 128]], channel_multiplier=0,
                                   allow_small_or_imprecise_dtypes=True), writes=["tmpi"])
        o("dve", lambda e: e.tensor_scalar(IDF[:, :], TMP[:, 0:128], IOP[:, 0:1], None,
                                           OP.is_equal), reads=["tmpi", "iop"], writes=["idf"])
        o("dve", lambda e: e.tensor_copy(IDB[:, :], IDF[:, :]), reads=["idf"], writes=["idb"])
        o("pool", lambda e: e.memset(ONEF[:, :], 1.0), writes=["onef"])
        o("pool", lambda e: e.memset(EMSK[:, :], 0.0), writes=["emsk"])
        for c in range(8):
            o("pool", lambda e, c=c: e.memset(EMSK[0:64, 16 * c + 2 * c:16 * c + 2 * c + 1], 1.0),
              writes=["emsk"])
            o("pool", lambda e, c=c: e.memset(EMSK[64:128, 16 * c + 2 * c + 1:16 * c + 2 * c + 2], 1.0),
              writes=["emsk"])
        dma("sq", xa.rearrange("p (t d) -> p t d", t=NT),
            x_in.rearrange("(t p) d -> p t d", p=128), writes=XKEYS)

    def emit_ln_to_xnT(affine=None, src_keys=None, dst_key="xnt"):
        skeys = src_keys or XKEYS
        for t in range(NT):
            xt = xa[:, t * DM:(t + 1) * DM]
            kx = skeys[t]
            sumc, sqc = SMALL[:, 0:1], SMALL[:, 1:2]
            junkD = tmp[:, 1024:1536].bitcast(BF16)
            junkA = tmp[:, 1536:2048].bitcast(BF16)
            o("dve", lambda e, xt=xt: e.tensor_scalar(junkD, xt, 1.0, 0.0, OP.mult, OP.add,
                                                      accum_out=sumc),
              reads=[kx], writes=["ln_sum", "jD"])
            o("act", lambda e, xt=xt: e.activation(junkA, xt, AF.Square, accum_out=sqc),
              reads=[kx], writes=["ln_sq", "jA"])
            mu, t2, mu2, ve = SMALL[:, 2:3], SMALL[:, 3:4], SMALL[:, 4:5], SMALL[:, 5:6]
            sd, rstd = SMALL[:, 6:7], SMALL[:, 7:8]
            o("dve", lambda e: e.tensor_scalar(mu, sumc, 1.0 / DM, None, OP.mult),
              reads=["ln_sum"], writes=["ln_mu"])
            o("dve", lambda e: e.tensor_scalar(t2, sqc, 1.0 / DM, 1e-5, OP.mult, OP.add),
              reads=["ln_sq"], writes=["ln_t2"])
            o("dve", lambda e: e.tensor_mul(mu2, mu, mu), reads=["ln_mu"], writes=["ln_mu2"])
            o("dve", lambda e: e.tensor_sub(ve, t2, mu2), reads=["ln_t2", "ln_mu2"], writes=["ln_ve"])
            o("act", lambda e: e.activation(sd, ve, AF.Sqrt), reads=["ln_ve"], writes=["ln_sd"])
            o("dve", lambda e: e.reciprocal(rstd, sd), reads=["ln_sd"], writes=["ln_rstd"])
            xnb = tmp[:, 0:512].bitcast(BF16)
            o("dve", lambda e, xt=xt: e.tensor_scalar(xnb, xt, mu, rstd, OP.subtract, OP.mult),
              reads=[kx, "ln_mu", "ln_rstd"], writes=["xnb"])
            for half in range(2):
                pb = 6 + half
                pview = psv(pb, 512)
                for j in range(4):
                    c = half * 4 + j
                    o("pe", lambda e, c=c, j=j, pview=pview: e.matmul(
                        pview[:, j * 128:(j + 1) * 128], xnb[:, c * 128:(c + 1) * 128],
                        IDB[:, :], start=(j == 0), stop=(j == 3), skip_group_check=True),
                      reads=["xnb", "idb"], writes=pk(pb))
                def evac(e, half=half, t=t, pview=pview):
                    d = bass.AP(A2, (half * 4) * S + t * 128, [[A2C, 128], [S, 4], [1, 128]])
                    return e.tensor_copy(d, pview[:, :])
                o("dve", evac, reads=pk(pb), writes=[dst_key])

    def xnt(c, s0, n):
        return a2[:, c * S + s0:c * S + s0 + n]

    def emit_moe_l0():
        emit_ln_to_xnT()
        ctx.barrier()
        WRB = 40960
        dma("pool", a2[:, WRB:WRB + 64].rearrange("p (c m) -> p c m", c=8),
            w2d(W["L0_ffn_Wr"], 0, DM, 0, 8), writes=["wrb"])
        for t in range(NT):
            pg = psv(0, 8)
            for c in range(NCH):
                o("pe", lambda e, t=t, c=c, pg=pg: e.matmul(
                    pg, xnt(c, t * 128, 128), a2[:, WRB + c * 8:WRB + c * 8 + 8],
                    start=(c == 0), stop=(c == NCH - 1)),
                  reads=["xnt", "wrb"], writes=pk(0))
            nm, sm, rs = SMALL[:, 8:9], SMALL[:, 9:10], SMALL[:, 10:11]
            m1, m2, dn, rd = SMALL[:, 11:12], SMALL[:, 12:13], SMALL[:, 13:14], SMALL[:, 14:15]
            pu, pr = SMALL[:, 16:24], SMALL[:, 24:32]
            eq1, tq, ge2, t1 = SMALL[:, 32:40], SMALL[:, 40:48], SMALL[:, 48:56], SMALL[:, 56:64]
            o("dve", lambda e, pg=pg: e.reduce_max(nm, pg, axis=AX, negate=True),
              reads=pk(0), writes=["g_nm"])
            o("act", lambda e, pg=pg: e.activation(pu, pg, AF.Exp, bias=nm),
              reads=pk(0) + ["g_nm"], writes=["g_pu"])
            o("dve", lambda e: e.reduce_sum(sm, pu, axis=AX), reads=["g_pu"], writes=["g_sm"])
            o("dve", lambda e: e.reciprocal(rs, sm), reads=["g_sm"], writes=["g_rs"])
            o("dve", lambda e: e.tensor_scalar(pr, pu, rs, None, OP.mult),
              reads=["g_pu", "g_rs"], writes=["g_pr"])
            o("dve", lambda e: e.reduce_max(m1, pr, axis=AX), reads=["g_pr"], writes=["g_m1"])
            o("dve", lambda e: e.tensor_scalar(eq1, pr, m1, 2.0, OP.is_equal, OP.mult),
              reads=["g_pr", "g_m1"], writes=["g_eq1"])
            o("dve", lambda e: e.tensor_sub(tq, pr, eq1), reads=["g_pr", "g_eq1"], writes=["g_tq"])
            o("dve", lambda e: e.reduce_max(m2, tq, axis=AX), reads=["g_tq"], writes=["g_m2"])
            o("dve", lambda e: e.tensor_scalar(ge2, pr, m2, None, OP.is_ge),
              reads=["g_pr", "g_m2"], writes=["g_ge2"])
            o("dve", lambda e: e.tensor_add(dn, m1, m2), reads=["g_m1", "g_m2"], writes=["g_dn"])
            o("dve", lambda e: e.reciprocal(rd, dn), reads=["g_dn"], writes=["g_rd"])
            o("dve", lambda e: e.tensor_scalar(t1, pr, rd, 0.5, OP.mult, OP.mult),
              reads=["g_pr", "g_rd"], writes=["g_t1"])
            o("dve", lambda e, t=t: e.tensor_mul(GAT[:, t * 10:t * 10 + 8], t1, ge2),
              reads=["g_t1", "g_ge2"], writes=["gat"])
        ctx.barrier()
        W1O, W3O, W2O = 16384, 24576, 32768
        def ldW13(e_i, qg):
            for nm_, off, wt in (("w1", W1O, W["L0_ffn_W1"]), ("w3", W3O, W["L0_ffn_W3"])):
                dma("pool", a2[:, off:off + 8192].rearrange("p (c m) -> p c m", c=8),
                    w3d(wt, e_i, 0, DM, qg * 1024, (qg + 1) * 1024), writes=[nm_])
        def ldW2(e_i, qg):
            dma("pool", a2[:, W2O:W2O + 8192].rearrange("p (c m) -> p c m", c=8),
                w3d(W["L0_ffn_W2"], e_i, qg * 1024, (qg + 1) * 1024, 0, DM), writes=["w2"])
        ldW13(0, 0); ldW2(0, 0)
        for nset in range(16):
            e_i, qg = nset // 2, nset % 2
            nxt = (nset + 1) // 2, (nset + 1) % 2
            for sb in range(4):
                s0 = sb * 512
                for h in range(8):
                    bs_ = h % 2
                    p1, p3 = psv(0 + bs_), psv(2 + bs_)
                    for c in range(NCH):
                        o("pe", lambda e, c=c, h=h, s0=s0, p1=p1: e.matmul(
                            p1, a2[:, W1O + c * 1024 + h * 128:W1O + c * 1024 + h * 128 + 128],
                            xnt(c, s0, 512), start=(c == 0), stop=(c == NCH - 1)),
                          reads=["w1", "xnt"], writes=pk(0 + bs_))
                    for c in range(NCH):
                        o("pe", lambda e, c=c, h=h, s0=s0, p3=p3: e.matmul(
                            p3, a2[:, W3O + c * 1024 + h * 128:W3O + c * 1024 + h * 128 + 128],
                            xnt(c, s0, 512), start=(c == 0), stop=(c == NCH - 1)),
                          reads=["w3", "xnt"], writes=pk(2 + bs_))
                    t1b = tmp[:, 512 + bs_ * 256:512 + bs_ * 256 + 256].bitcast(BF16)
                    o("act", lambda e, p1=p1, t1b=t1b: e.activation(t1b, p1, AF.Silu),
                      reads=pk(0 + bs_), writes=["t1b%d" % bs_])
                    o("dve", lambda e, p3=p3, t1b=t1b, h=h: e.tensor_tensor(
                        a3[:, h * 512:h * 512 + 512], t1b, p3, OP.mult),
                      reads=["t1b%d" % bs_] + pk(2 + bs_), writes=["hc%d" % h])
                if nset < 15 and sb == 3:
                    ldW13(*nxt)
                for m in range(4):
                    t_t = sb * 4 + m
                    for n in range(2):
                        bsel = 4 + (m * 2 + n) % 2
                        po = psv(bsel)
                        for h in range(8):
                            o("pe", lambda e, h=h, m=m, n=n, po=po: e.matmul(
                                po, a3[:, h * 512 + m * 128:h * 512 + m * 128 + 128],
                                a2[:, W2O + h * 1024 + n * 512:W2O + h * 1024 + n * 512 + 512],
                                start=(h == 0), stop=(h == 7)),
                              reads=["hc%d" % h, "w2"], writes=pk(bsel))
                        o("dve", lambda e, po=po, t_t=t_t, n=n, e_i=e_i: e.scalar_tensor_tensor(
                            xa[:, t_t * DM + n * 512:t_t * DM + n * 512 + 512], po,
                            GAT[:, t_t * 10 + e_i:t_t * 10 + e_i + 1],
                            xa[:, t_t * DM + n * 512:t_t * DM + n * 512 + 512],
                            OP.mult, OP.add),
                          reads=pk(bsel) + ["gat", "xa%d" % t_t], writes=["xa%d" % t_t])
                if nset < 15 and sb == 3:
                    ldW2(*nxt)

    def emit_attn_l0():
        emit_ln_to_xnT()
        dma("sq", xs[:, :], xa[:, :], reads=XKEYS, writes=["xspill"])
        ctx.barrier()
        QTO, KTO = 0, 16384
        WQB, WKB, WVB = 16384, 24576, 32768
        DSL, SLAB, WOO, EXPO = 0, 16384, 24576, 32768
        ACOL, NCCOL = 64, 67
        dma("sq", SMALL[0:1, 0:1], W["L0_attn_mix_weight"][0:1, 0:1], writes=["mixs"])
        dma("sq", SMALL[0:1, 1:2], W["L0_attn_decay_factor"][0:1, 0:1], writes=["decs"])
        o("act", lambda e: e.activation(SMALL[0:1, 2:3], SMALL[0:1, 0:1], AF.Sigmoid),
          reads=["mixs"], writes=["sigm"])
        prep = psv(7, 128)
        o("pe", lambda e: e.matmul(prep[:, 0:1], ONEF[0:1, 0:128], SMALL[0:1, 2:3],
                                   start=True, stop=False, skip_group_check=True),
          reads=["sigm", "onef"], writes=pk(7))
        o("pe", lambda e: e.matmul(prep[:, 1:2], ONEF[0:1, 0:128], SMALL[0:1, 1:2],
                                   start=False, stop=True, skip_group_check=True),
          reads=["decs", "onef"], writes=pk(7))
        o("dve", lambda e: e.tensor_copy(BCOL[:, ACOL:ACOL + 2], prep[:, 0:2]),
          reads=pk(7), writes=["acol"])
        o("dve", lambda e: e.tensor_scalar(BCOL[:, NCCOL + 1:NCCOL + 2],
                                           BCOL[:, ACOL + 1:ACOL + 2], 1.0, None, OP.subtract),
          reads=["acol"], writes=["ncol"])

        def proj_feature_major(wslot, wkey, dsto, dkey):
            for co in range(8):
                for nb in range(4):
                    pb = (co * 4 + nb) % 2
                    pq_ = psv(pb)
                    for c in range(NCH):
                        o("pe", lambda e, co=co, nb=nb, c=c, pq_=pq_, wslot=wslot: e.matmul(
                            pq_, a2[:, wslot + c * 1024 + co * 128:wslot + c * 1024 + co * 128 + 128],
                            a2[:, c * S + nb * 512:c * S + nb * 512 + 512],
                            start=(c == 0), stop=(c == NCH - 1)),
                          reads=[wkey, "xnt"], writes=pk(pb))
                    o("dve", lambda e, co=co, nb=nb, pq_=pq_, dsto=dsto: e.tensor_copy(
                        xab[:, dsto + co * 2048 + nb * 512:dsto + co * 2048 + nb * 512 + 512], pq_),
                      reads=pk(pb), writes=[dkey])

        def proj_v(wslot, wkey):
            o("pool", lambda e: e.memset(bass.AP(A3, 64, [[NT * 1040, 128], [65, 256]]), 1.0),
              writes=["vsb"])
            for t in range(NT):
                for nb in range(2):
                    pb = 2 + (t * 2 + nb) % 2
                    pq_ = psv(pb)
                    for c in range(NCH):
                        o("pe", lambda e, t=t, nb=nb, c=c, pq_=pq_, wslot=wslot: e.matmul(
                            pq_, a2[:, c * S + t * 128:c * S + t * 128 + 128],
                            a2[:, wslot + c * 1024 + nb * 512:wslot + c * 1024 + nb * 512 + 512],
                            start=(c == 0), stop=(c == NCH - 1)),
                          reads=[wkey, "xnt"], writes=pk(pb))
                    def evv(e, t=t, nb=nb, pq_=pq_):
                        d = bass.AP(A3, t * 1040 + nb * 8 * 65, [[NT * 1040, 128], [65, 8], [1, 64]])
                        return e.tensor_copy(d, pq_.rearrange("p (h j) -> p h j", h=8))
                    o("dve", evv, reads=pk(pb), writes=["vsb"])

        def cnorms(srco, srckey, outcol):
            for c in range(NCH):
                sq = tmp[:, 0:1024].bitcast(BF16)
                o("dve", lambda e, c=c, srco=srco: e.tensor_tensor(
                    sq, xab[:, srco + c * 2048:srco + c * 2048 + 2048],
                    xab[:, srco + c * 2048:srco + c * 2048 + 2048], OP.mult),
                  reads=[srckey], writes=["csq"])
                for nb in range(4):
                    o("pe", lambda e, c=c, nb=nb: e.matmul(
                        ps[0:16, nb * 512:nb * 512 + 512], EMSK[:, 16 * c:16 * c + 16],
                        sq[:, nb * 512:nb * 512 + 512],
                        start=(c == 0), stop=(c == NCH - 1), skip_group_check=True),
                      reads=["csq", "emsk"], writes=pk(nb))
            o("dve", lambda e: e.reduce_max(SMALL[0:16, outcol:outcol + 1], ps[0:16, 0:2048],
                                            axis=AX),
              reads=pk(0) + pk(1) + pk(2) + pk(3), writes=["cn%d" % outcol])

        def negc_rep(col_q, col_k, dstcol, key):
            o("dve", lambda e: e.tensor_mul(SMALL[0:16, 34:35], SMALL[0:16, col_q:col_q + 1],
                                            SMALL[0:16, col_k:col_k + 1]),
              reads=["cn%d" % col_q, "cn%d" % col_k], writes=["cprod"])
            o("act", lambda e: e.activation(SMALL[0:16, 35:36], SMALL[0:16, 34:35], AF.Sqrt,
                                            scale=1.0 / 64.0),
              reads=["cprod"], writes=["csqr"])
            o("dve", lambda e: e.tensor_scalar(SMALL[0:16, 36:37], SMALL[0:16, 35:36],
                                               -1.0, None, OP.mult),
              reads=["csqr"], writes=["cneg"])
            pb = psv(6, 128)
            o("pe", lambda e: e.matmul(pb[0:1, 0:16], SMALL[0:16, 36:37], IDF[0:16, 0:16],
                                       start=True, stop=True, skip_group_check=True),
              reads=["cneg", "idf"], writes=pk(6))
            o("dve", lambda e: e.tensor_copy(TMP[0:1, 2048:2064], pb[0:1, 0:16]),
              reads=pk(6), writes=["crow"])
            pb2 = psv(7, 128)
            o("pe", lambda e: e.matmul(pb2[:, 0:16], ONEF[0:1, 0:128], TMP[0:1, 2048:2064],
                                       start=True, stop=True, skip_group_check=True),
              reads=["crow", "onef"], writes=pk(7))
            o("dve", lambda e, dstcol=dstcol: e.tensor_copy(BCOL[:, dstcol:dstcol + 16],
                                                            pb2[:, 0:16]),
              reads=pk(7), writes=[key])

        dma("pool", a2[:, WQB:WQB + 8192].rearrange("p (c m) -> p c m", c=8),
            w2d(W["L0_attn_Wq"], 0, DM, 0, DM), writes=["wqb"])
        proj_feature_major(WQB, "wqb", QTO, "qt")
        cnorms(QTO, "qt", 32)
        dma("pool", a2[:, WKB:WKB + 8192].rearrange("p (c m) -> p c m", c=8),
            w2d(W["L0_attn_Wkv_long"], 0, DM, 0, DM), writes=["wkb"])
        proj_feature_major(WKB, "wkb", KTO, "kt")
        cnorms(KTO, "kt", 33)
        negc_rep(32, 33, 112, "negCL")
        dma("pool", a2[:, WVB:WVB + 8192].rearrange("p (c m) -> p c m", c=8),
            w2d(W["L0_attn_Wkv_long"], 0, DM, DM, 2 * DM), writes=["wvb"])
        proj_v(WVB, "wvb")
        dma("sq", ktl_s[:, :], xab[:, KTO:KTO + 16384], reads=["kt"], writes=["ktspill"])
        dma("sq", vl_s[:, :], a3[:, 0:16640], reads=["vsb"], writes=["vlspill"])
        ctx.barrier()
        dma("pool", a2[:, WKB:WKB + 8192].rearrange("p (c m) -> p c m", c=8),
            w2d(W["L0_attn_Wkv_short"], 0, DM, 0, DM), writes=["wkb"])
        proj_feature_major(WKB, "wkb", KTO, "kt")
        cnorms(KTO, "kt", 33)
        negc_rep(32, 33, 96, "negCS")
        dma("pool", a2[:, WVB:WVB + 8192].rearrange("p (c m) -> p c m", c=8),
            w2d(W["L0_attn_Wkv_short"], 0, DM, DM, 2 * DM), writes=["wvb"])
        proj_v(WVB, "wvb")
        ctx.barrier()
        dma("pool", a2[:, WOO:WOO + 8192].rearrange("p (c m) -> p c m", c=8),
            w2d(W["L0_attn_Wo"], 0, DM, 0, DM), writes=["wo"])

        def attn_phase(is_short):
            ncc = 96 if is_short else 112
            nck = "negCS" if is_short else "negCL"
            for qb in (0, 1024):
                if is_short:
                    for kc in range(NT):
                        d0 = float(kc * 128 - qb)
                        o("dve", lambda e, d0=d0: e.tensor_scalar(SMALL[:, 40:41], IOP[:, 0:1],
                                                                  d0, None, OP.add),
                          reads=["iop"], writes=["pdc"])
                        o("dve", lambda e: e.tensor_scalar(TMP[:, 0:1024], IOF[:, 0:1024],
                                                           SMALL[:, 40:41],
                                                           BCOL[:, NCCOL + 1:NCCOL + 2],
                                                           OP.subtract, OP.mult),
                          reads=["iof", "pdc", "ncol"], writes=["dabs"])
                        o("act", lambda e: e.activation(TMP[:, 0:1024], TMP[:, 0:1024], AF.Abs),
                          reads=["dabs"], writes=["dabs"])
                        o("act", lambda e, kc=kc: e.activation(
                            a2[:, DSL + kc * 1024:DSL + kc * 1024 + 1024], TMP[:, 0:1024],
                            AF.Exp, scale=-1.0),
                          reads=["dabs"], writes=["dsl"])
                for h in range(16):
                    hc, hb = h // 2, 64 * (h % 2)
                    avb = 4 + 2 * (h % 2)
                    for kc in range(NT):
                        qkb = 2 * (kc % 2)
                        pq_ = psv(qkb, 1024)
                        for nb in range(2):
                            o("pe", lambda e, hc=hc, hb=hb, kc=kc, nb=nb, qb=qb, pq_=pq_: e.matmul(
                                pq_[:, nb * 512:nb * 512 + 512],
                                xab[hb:hb + 64, KTO + hc * 2048 + kc * 128:KTO + hc * 2048 + kc * 128 + 128],
                                xab[hb:hb + 64, QTO + hc * 2048 + qb + nb * 512:QTO + hc * 2048 + qb + nb * 512 + 512],
                                start=True, stop=True, skip_group_check=True),
                              reads=["kt", "qt"], writes=pk(qkb, 1024))
                        eslot = EXPO + (kc % 2) * 1024
                        o("act", lambda e, pq_=pq_, eslot=eslot, h=h, ncc=ncc: e.activation(
                            a2[:, eslot:eslot + 1024], pq_[:, 0:1024], AF.Exp,
                            bias=BCOL[:, ncc + h:ncc + h + 1], scale=0.125),
                          reads=pk(qkb, 1024) + [nck], writes=["exp%d" % (kc % 2)])
                        if is_short:
                            o("dve", lambda e, eslot=eslot, kc=kc: e.tensor_tensor(
                                a2[:, eslot:eslot + 1024], a2[:, eslot:eslot + 1024],
                                a2[:, DSL + kc * 1024:DSL + kc * 1024 + 1024], OP.mult),
                              reads=["exp%d" % (kc % 2), "dsl"], writes=["exp%d" % (kc % 2)])
                        for m in range(8):
                            o("pe", lambda e, eslot=eslot, m=m, kc=kc, h=h, avb=avb: e.matmul(
                                ps[:, (avb + m // 4) * 512 + (m % 4) * 65:
                                   (avb + m // 4) * 512 + (m % 4) * 65 + 65],
                                a2[:, eslot + m * 128:eslot + m * 128 + 128],
                                a3[:, kc * 1040 + 65 * h:kc * 1040 + 65 * h + 65],
                                start=(kc == 0), stop=(kc == NT - 1), skip_group_check=True),
                              reads=["exp%d" % (kc % 2), "vsb"], writes=pk(avb + m // 4))
                    for m in range(8):
                        base = (avb + m // 4) * 512 + (m % 4) * 65
                        o("dve", lambda e, base=base: e.tensor_scalar(
                            SMALL[:, 41:42], ps[:, base + 64:base + 65], 1e-30, None, OP.max),
                          reads=pk(avb + m // 4), writes=["dmx"])
                        o("dve", lambda e: e.reciprocal(SMALL[:, 42:43], SMALL[:, 41:42]),
                          reads=["dmx"], writes=["rden"])
                        o("dve", lambda e, base=base, m=m, h=h: e.tensor_scalar(
                            a2[:, SLAB + m * 1024 + 64 * h:SLAB + m * 1024 + 64 * h + 64],
                            ps[:, base:base + 64], SMALL[:, 42:43], None, OP.mult),
                          reads=pk(avb + m // 4) + ["rden"], writes=["slab"])
                for m in range(8):
                    for g in range(2):
                        pb_ = g % 2
                        pview = psv(pb_, 512)
                        for j in range(4):
                            dc = g * 4 + j
                            o("pe", lambda e, m=m, dc=dc, j=j, pview=pview: e.matmul(
                                pview[:, j * 128:(j + 1) * 128],
                                a2[:, SLAB + m * 1024 + dc * 128:SLAB + m * 1024 + dc * 128 + 128],
                                IDB[:, :], start=(j == 0), stop=(j == 3), skip_group_check=True),
                              reads=["slab", "idb"], writes=pk(pb_))
                        def evt(e, m=m, g=g, pview=pview):
                            d = bass.AP(A2, DSL + (g * 4) * 1024 + m * 128,
                                        [[A2C, 128], [1024, 4], [1, 128]])
                            return e.tensor_copy(d, pview[:, :])
                        o("dve", evt, reads=pk(pb_), writes=["dsl"])
                for m in range(8):
                    t_t = qb // 128 + m
                    dma("sq", TMP[:, 1024:2048], xs[:, t_t * DM:(t_t + 1) * DM],
                        writes=["xl"], ch="xl")
                    yt = tmp[:, 2048:3072]
                    for n in range(2):
                        pb_ = 2 + n
                        po = psv(pb_)
                        for c in range(NCH):
                            o("pe", lambda e, m=m, n=n, c=c, po=po: e.matmul(
                                po, a2[:, DSL + c * 1024 + m * 128:DSL + c * 1024 + m * 128 + 128],
                                a2[:, WOO + c * 1024 + n * 512:WOO + c * 1024 + n * 512 + 512],
                                start=(c == 0), stop=(c == NCH - 1)),
                              reads=["dsl", "wo"], writes=pk(pb_))
                        acol = ACOL if is_short else ACOL + 2
                        o("dve", lambda e, po=po, n=n, acol=acol: e.scalar_tensor_tensor(
                            yt[:, n * 512:n * 512 + 512], po, BCOL[:, acol:acol + 1],
                            TMP[:, 1024 + n * 512:1024 + n * 512 + 512], OP.mult, OP.add),
                          reads=pk(pb_) + ["acol2", "xl"], writes=["yt"])
                    dma("sq", xs[:, t_t * DM:(t_t + 1) * DM], yt, reads=["yt"],
                        writes=["xl"], ch="xl")

        o("dve", lambda e: e.tensor_scalar(BCOL[:, ACOL + 2:ACOL + 3], BCOL[:, ACOL:ACOL + 1],
                                           -1.0, 1.0, OP.mult, OP.add),
          reads=["acol"], writes=["acol2"])
        o("dve", lambda e: e.tensor_copy(SMALL[:, 43:44], BCOL[:, ACOL:ACOL + 1]),
          reads=["acol"], writes=["acol2"])
        attn_phase(True)
        ctx.barrier()
        dma("sq", xab[:, KTO:KTO + 16384], ktl_s[:, :], reads=["ktspill"], writes=["kt"])
        dma("sq", a3[:, 0:16640], vl_s[:, :], reads=["vlspill"], writes=["vsb"])
        ctx.barrier()
        attn_phase(False)
        ctx.barrier()
        dma("sq", xa[:, :], xs[:, :], reads=["xl"], writes=XKEYS)
        ctx.barrier()

    def emit_moe_l1():
        emit_ln_to_xnT()
        ctx.barrier()
        W1O, W2O, R1O, WR1B, WR2B = 16384, 24576, 32768, 36864, 38912
        def bias_to_cols(src_flat, n, col0, key):
            m = n // 128
            dma("sq", TMP[0:m, 0:128], src_flat.rearrange("(r p) -> r p", p=128),
                writes=["brows"], ch="brows")
            pt = psv(7, 128)
            o("pe", lambda e, m=m: e.matmul(pt[:, 0:m], TMP[0:m, 0:128], IDF[0:m, 0:m],
                                            start=True, stop=True, skip_group_check=True),
              reads=["brows", "idf"], writes=pk(7))
            o("dve", lambda e, m=m, col0=col0: e.tensor_copy(BCOL[:, col0:col0 + m], pt[:, 0:m]),
              reads=pk(7), writes=[key])
        GTv = TMP[0:16, 1792:3840]
        bias_to_cols(W["L1_ffn_br1"][:], 256, 32, "br1c")
        dma("sq", SMALL[0:1, 56:64], W["L1_ffn_br2"][None, :], writes=["br2r"])
        dma("sq", BMAT[0:8, 0:1024], W["L1_ffn_be2"][:, :], writes=["bmat"])
        dma("sq", BMAT[8:9, 0:1024], W["L1_ffn_bs2"][None, :], writes=["bmat"])
        o("pool", lambda e: e.memset(bass.AP(GAT, 8, [[NT * 10, 128], [10, NT], [1, 1]]), 0.5),
          writes=["gat"])
        dma("pool", a2[:, WR1B:WR1B + 2048].rearrange("p (c m) -> p c m", c=8),
            w2d(W["L1_ffn_Wr1"], 0, DM, 0, 256), writes=["wr1b"])
        dma("pool", a2[:, WR2B:WR2B + 16].rearrange("p (c m) -> p c m", c=2),
            w2d(W["L1_ffn_Wr2"], 0, 256, 0, 8), writes=["wr2b"])
        for co in range(2):
            for nb in range(4):
                pb_ = (co * 4 + nb) % 2
                pq_ = psv(pb_)
                for c in range(NCH):
                    o("pe", lambda e, co=co, nb=nb, c=c, pq_=pq_: e.matmul(
                        pq_, a2[:, WR1B + c * 256 + co * 128:WR1B + c * 256 + co * 128 + 128],
                        a2[:, c * S + nb * 512:c * S + nb * 512 + 512],
                        start=(c == 0), stop=(c == NCH - 1)),
                      reads=["wr1b", "xnt"], writes=pk(pb_))
                o("act", lambda e, co=co, nb=nb, pq_=pq_: e.activation(
                    a2[:, R1O + co * 2048 + nb * 512:R1O + co * 2048 + nb * 512 + 512], pq_,
                    AF.Relu, bias=BCOL[:, 32 + co:32 + co + 1]),
                  reads=pk(pb_) + ["br1c"], writes=["r1t"])
        for t in range(NT):
            pg = psv(0, 8)
            for co in range(2):
                o("pe", lambda e, t=t, co=co, pg=pg: e.matmul(
                    pg, a2[:, R1O + co * 2048 + t * 128:R1O + co * 2048 + t * 128 + 128],
                    a2[:, WR2B + co * 8:WR2B + co * 8 + 8],
                    start=(co == 0), stop=False),
                  reads=["r1t", "wr2b"], writes=pk(0))
            o("pe", lambda e, t=t, pg=pg: e.matmul(
                pg, ONEF[0:1, 0:128], SMALL[0:1, 56:64], start=False, stop=True),
              reads=["br2r", "onef"], writes=pk(0))
            nm, sm, rs = SMALL[:, 8:9], SMALL[:, 9:10], SMALL[:, 10:11]
            m1, m2 = SMALL[:, 11:12], SMALL[:, 12:13]
            pu, pr = SMALL[:, 16:24], SMALL[:, 24:32]
            eq1, t2, tq = SMALL[:, 32:40], SMALL[:, 40:48], SMALL[:, 48:56]
            eq2, wa, wb_ = BCOL[:, 136:144], BCOL[:, 144:152], BCOL[:, 152:160]
            o("dve", lambda e, pg=pg: e.reduce_max(nm, pg, axis=AX, negate=True),
              reads=pk(0), writes=["g_nm"])
            o("act", lambda e, pg=pg: e.activation(pu, pg, AF.Exp, bias=nm),
              reads=pk(0) + ["g_nm"], writes=["g_pu"])
            o("dve", lambda e: e.reduce_sum(sm, pu, axis=AX), reads=["g_pu"], writes=["g_sm"])
            o("dve", lambda e: e.reciprocal(rs, sm), reads=["g_sm"], writes=["g_rs"])
            o("dve", lambda e: e.tensor_scalar(pr, pu, rs, None, OP.mult),
              reads=["g_pu", "g_rs"], writes=["g_pr"])
            o("dve", lambda e: e.reduce_max(m1, pr, axis=AX), reads=["g_pr"], writes=["g_m1"])
            o("dve", lambda e: e.tensor_scalar(eq1, pr, m1, None, OP.is_equal),
              reads=["g_pr", "g_m1"], writes=["g_eq1"])
            o("dve", lambda e: e.tensor_scalar(t2, eq1, 2.0, None, OP.mult),
              reads=["g_eq1"], writes=["g_t2"])
            o("dve", lambda e: e.tensor_sub(tq, pr, t2), reads=["g_pr", "g_t2"], writes=["g_tq"])
            o("dve", lambda e: e.reduce_max(m2, tq, axis=AX), reads=["g_tq"], writes=["g_m2"])
            o("dve", lambda e: e.tensor_scalar(eq2, pr, m2, None, OP.is_equal),
              reads=["g_pr", "g_m2"], writes=["g_eq2"])
            d21 = SMALL[:, 13:14]
            o("dve", lambda e: e.tensor_sub(d21, m1, m2), reads=["g_m1", "g_m2"], writes=["g_d21"])
            tg1, tg2 = SMALL[:, 14:15], SMALL[:, 15:16]
            o("act", lambda e: e.activation(tg1, d21, AF.Sigmoid, scale=0.5),
              reads=["g_d21"], writes=["g_tg1"])
            o("dve", lambda e: e.tensor_scalar(tg1, tg1, 0.5, None, OP.mult),
              reads=["g_tg1"], writes=["g_tg1"])
            o("dve", lambda e: e.tensor_scalar(tg2, tg1, -1.0, 0.5, OP.mult, OP.add),
              reads=["g_tg1"], writes=["g_tg2"])
            o("dve", lambda e: e.tensor_scalar(wa, eq1, tg1, None, OP.mult),
              reads=["g_eq1", "g_tg1"], writes=["g_wa"])
            o("dve", lambda e: e.tensor_scalar(wb_, eq2, tg2, None, OP.mult),
              reads=["g_eq2", "g_tg2"], writes=["g_wb"])
            wc_ = SMALL[:, 40:48]
            o("dve", lambda e: e.tensor_scalar(wc_, eq1, -1.0, 1.0, OP.mult, OP.add),
              reads=["g_eq1"], writes=["g_t2"])
            o("dve", lambda e: e.tensor_mul(wb_, wb_, wc_), reads=["g_wb", "g_t2"], writes=["g_wb"])
            o("dve", lambda e, t=t: e.tensor_add(GAT[:, t * 10:t * 10 + 8], wa, wb_),
              reads=["g_wa", "g_wb"], writes=["gat"])
        for t0 in range(0, NT, 4):
            pb_ = psv(7, 512)
            for j in range(4):
                t = t0 + j
                o("pe", lambda e, t=t, j=j, pb_=pb_: e.matmul(
                    pb_[0:9, j * 128:j * 128 + 128], GAT[:, t * 10:t * 10 + 9], IDF[:, 0:128],
                    start=(j == 0), stop=(j == 3), skip_group_check=True),
                  reads=["gat", "idf"], writes=pk(7))
            o("dve", lambda e, t0=t0, pb_=pb_: e.tensor_copy(GTv[0:9, t0 * 128:t0 * 128 + 512],
                                                             pb_[0:9, 0:512]),
              reads=pk(7), writes=["gt"])
        for t in range(NT):
            for n in range(2):
                pb_ = 2 + (t * 2 + n) % 2
                po = psv(pb_)
                o("pe", lambda e, t=t, n=n, po=po: e.matmul(
                    po, GTv[0:9, t * 128:t * 128 + 128], BMAT[0:9, n * 512:n * 512 + 512],
                    start=True, stop=True),
                  reads=["gt", "bmat"], writes=pk(pb_))
                o("dve", lambda e, po=po, t=t, n=n: e.tensor_tensor(
                    xa[:, t * DM + n * 512:t * DM + n * 512 + 512],
                    xa[:, t * DM + n * 512:t * DM + n * 512 + 512], po, OP.add),
                  reads=pk(pb_) + ["xa%d" % t], writes=["xa%d" % t])
        ctx.barrier()
        def l1_srcs(ei):
            if ei < 8:
                return (lambda qg: w3d(W["L1_ffn_We1"], ei, 0, DM, qg * 1024, (qg + 1) * 1024),
                        lambda qg: w3d(W["L1_ffn_We2"], ei, qg * 1024, (qg + 1) * 1024, 0, DM),
                        W["L1_ffn_be1"][ei:ei + 1, :].rearrange("a (r p) -> (a r) p", p=128))
            return (lambda qg: w2d(W["L1_ffn_Ws1"], 0, DM, qg * 1024, (qg + 1) * 1024),
                    lambda qg: w2d(W["L1_ffn_Ws2"], qg * 1024, (qg + 1) * 1024, 0, DM),
                    W["L1_ffn_bs1"][:].rearrange("(r p) -> r p", p=128))

        def l1_ldW1(ei, qg):
            dma("pool", a2[:, W1O:W1O + 8192].rearrange("p (c m) -> p c m", c=8),
                l1_srcs(ei)[0](qg), writes=["w1"])
        def l1_ldW2(ei, qg):
            dma("pool", a2[:, W2O:W2O + 8192].rearrange("p (c m) -> p c m", c=8),
                l1_srcs(ei)[1](qg), writes=["w2"])
        def l1_ldb(ei):
            dma("sq", TMP[0:32, 0:128], l1_srcs(ei)[2], writes=["brows"], ch="brows")
            pt = psv(7, 128)
            o("pe", lambda e: e.matmul(pt[:, 0:32], TMP[0:32, 0:128], IDF[0:32, 0:32],
                                       start=True, stop=True, skip_group_check=True),
              reads=["brows", "idf"], writes=pk(7))
            o("dve", lambda e: e.tensor_copy(BCOL[:, 34:66], pt[:, 0:32]),
              reads=pk(7), writes=["be1c"])

        l1_ldW1(0, 0); l1_ldW2(0, 0); l1_ldb(0)
        for nset in range(36):
            ei, qg = nset // 4, nset % 4
            for sb in range(4):
                s0 = sb * 512
                for h in range(8):
                    bs_ = h % 2
                    p1 = psv(0 + bs_)
                    for c in range(NCH):
                        o("pe", lambda e, c=c, h=h, s0=s0, p1=p1: e.matmul(
                            p1, a2[:, W1O + c * 1024 + h * 128:W1O + c * 1024 + h * 128 + 128],
                            a2[:, c * S + s0:c * S + s0 + 512],
                            start=(c == 0), stop=(c == NCH - 1)),
                          reads=["w1", "xnt"], writes=pk(0 + bs_))
                    o("act", lambda e, p1=p1, h=h, qg=qg: e.activation(
                        a3[:, h * 512:h * 512 + 512], p1, AF.Silu,
                        bias=BCOL[:, 34 + qg * 8 + h:34 + qg * 8 + h + 1]),
                      reads=pk(0 + bs_) + ["be1c"], writes=["hc%d" % h])
                if sb == 3 and nset < 35:
                    l1_ldW1((nset + 1) // 4, (nset + 1) % 4)
                    if (nset + 1) % 4 == 0:
                        l1_ldb((nset + 1) // 4)
                for m in range(4):
                    t_t = sb * 4 + m
                    for n in range(2):
                        bsel = 4 + (m * 2 + n) % 2
                        po = psv(bsel)
                        for h in range(8):
                            o("pe", lambda e, h=h, m=m, n=n, po=po: e.matmul(
                                po, a3[:, h * 512 + m * 128:h * 512 + m * 128 + 128],
                                a2[:, W2O + h * 1024 + n * 512:W2O + h * 1024 + n * 512 + 512],
                                start=(h == 0), stop=(h == 7)),
                              reads=["hc%d" % h, "w2"], writes=pk(bsel))
                        o("dve", lambda e, po=po, t_t=t_t, n=n, ei=ei: e.scalar_tensor_tensor(
                            xa[:, t_t * DM + n * 512:t_t * DM + n * 512 + 512], po,
                            GAT[:, t_t * 10 + ei:t_t * 10 + ei + 1],
                            xa[:, t_t * DM + n * 512:t_t * DM + n * 512 + 512],
                            OP.mult, OP.add),
                          reads=pk(bsel) + ["gat", "xa%d" % t_t], writes=["xa%d" % t_t])
                if sb == 3 and nset < 35:
                    l1_ldW2((nset + 1) // 4, (nset + 1) % 4)

    def tk(a, b):
        return ["T%d" % i for i in range(a // 256, (b + 255) // 256)]

    def emit_attn_l1():
        emit_ln_to_xnT()
        ctx.barrier()
        QPT, WB, FTO, WGB0, WGB1 = 16384, 32768, 16384, 38912, 40960
        KVO = 32768
        dma("sq", xs[:, :], xa[:, :], reads=XKEYS, writes=["xospill"])
        ctx.barrier()

        def cols_from(src128, m, col0, key, ragged_tail=0):
            if ragged_tail:
                o("pool", lambda e, m=m: e.memset(TMP[0:m, 0:128], 0.0), writes=tk(0, 128))
                full = (m - 1) * 128
                dma("sq", TMP[0:m - 1, 0:128], src128[0:full].rearrange("(r p) -> r p", p=128),
                    writes=tk(0, 128), ch="brows")
                dma("sq", TMP[m - 1:m, 0:ragged_tail], src128[None, full:full + ragged_tail],
                    writes=tk(0, 128), ch="brows")
            else:
                dma("sq", TMP[0:m, 0:128], src128.rearrange("(r p) -> r p", p=128),
                    writes=tk(0, 128), ch="brows")
            pt = psv(7, 128)
            o("pe", lambda e, m=m: e.matmul(pt[:, 0:m], TMP[0:m, 0:128], IDF[0:m, 0:m],
                                            start=True, stop=True, skip_group_check=True),
              reads=tk(0, 128) + ["idf"], writes=pk(7))
            o("dve", lambda e, m=m, col0=col0: e.tensor_copy(BCOL[:, col0:col0 + m], pt[:, 0:m]),
              reads=pk(7), writes=[key])

        cols_from(W["L1_attn_bqkv"][0:DM], 8, 0, "bqc")
        o("dve", lambda e: e.tensor_scalar(BCOL[:, 8:16], BCOL[:, 0:8], -1.0, None, OP.mult),
          reads=["bqc"], writes=["nbqc"])
        dma("pool", a2[:, WB:WB + 8192].rearrange("p (c m) -> p c m", c=8),
            w2d(W["L1_attn_Wqkv"], 0, DM, 0, DM), writes=["wqkv"])
        scr0, scr1 = TMP[:, 0:512], TMP[:, 512:1024]
        K0, K1 = tk(0, 512), tk(512, 1024)
        for co in range(8):
            for nb in range(4):
                pb_ = (co * 4 + nb) % 2
                pq_ = psv(pb_)
                for c in range(NCH):
                    o("pe", lambda e, co=co, c=c, nb=nb, pq_=pq_: e.matmul(
                        pq_, a2[:, WB + c * 1024 + co * 128:WB + c * 1024 + co * 128 + 128],
                        a2[:, c * S + nb * 512:c * S + nb * 512 + 512],
                        start=(c == 0), stop=(c == NCH - 1)),
                      reads=["wqkv", "xnt"], writes=pk(pb_))
                o("act", lambda e, pq_=pq_, co=co: e.activation(
                    scr0, pq_, AF.Relu, bias=BCOL[:, co:co + 1]),
                  reads=pk(pb_) + ["bqc"], writes=K0)
                o("act", lambda e, pq_=pq_, co=co: e.activation(
                    scr1, pq_, AF.Relu, bias=BCOL[:, 8 + co:8 + co + 1], scale=-1.0),
                  reads=pk(pb_) + ["nbqc"], writes=K1)
                o("act", lambda e: e.activation(scr1, scr1, AF.Exp, scale=-1.0),
                  reads=K1, writes=K1)
                o("dve", lambda e, co=co, nb=nb: e.tensor_tensor(
                    a2[:, QPT + co * 2048 + nb * 512:QPT + co * 2048 + nb * 512 + 512],
                    scr0, scr1, OP.add),
                  reads=K0 + K1, writes=["qpt"])
        if L1CUT < 1:
            return
        dma("sq", TMP[0:1, 1024:2048], W["L1_attn_bqkv"][None, DM:2 * DM],
            writes=tk(1024, 2048), ch="brows")
        dma("sq", TMP[0:1, 2560:3584], W["L1_attn_bqkv"][None, 2 * DM:3 * DM],
            writes=tk(2560, 3584), ch="brows")
        dma("pool", a2[:, WB:WB + 8192].rearrange("p (c m) -> p c m", c=8),
            w2d(W["L1_attn_Wqkv"], 0, DM, DM, 2 * DM), writes=["wqkv"])
        kpkeys = ["kp"] + ["xa%d" % t for t in range(8)]
        for t in range(NT):
            for nb in range(2):
                pb_ = 2 + (t * 2 + nb) % 2
                pq_ = psv(pb_)
                for c in range(NCH):
                    o("pe", lambda e, t=t, nb=nb, c=c, pq_=pq_: e.matmul(
                        pq_, a2[:, c * S + t * 128:c * S + t * 128 + 128],
                        a2[:, WB + c * 1024 + nb * 512:WB + c * 1024 + nb * 512 + 512],
                        start=(c == 0), stop=False),
                      reads=["wqkv", "xnt"], writes=pk(pb_))
                o("pe", lambda e, nb=nb, pq_=pq_: e.matmul(
                    pq_, ONEF[0:1, 0:128], TMP[0:1, 1024 + nb * 512:1024 + nb * 512 + 512],
                    start=False, stop=True),
                  reads=tk(1024, 2048) + ["onef"], writes=pk(pb_))
                o("act", lambda e, pq_=pq_: e.activation(scr0, pq_, AF.Relu),
                  reads=pk(pb_), writes=K0)
                o("act", lambda e, pq_=pq_: e.activation(scr1, pq_, AF.Relu, scale=-1.0),
                  reads=pk(pb_), writes=K1)
                o("act", lambda e: e.activation(scr1, scr1, AF.Exp, scale=-1.0),
                  reads=K1, writes=K1)
                o("dve", lambda e, t=t, nb=nb: e.tensor_tensor(
                    xab[:, t * 1024 + nb * 512:t * 1024 + nb * 512 + 512],
                    scr0, scr1, OP.add),
                  reads=K0 + K1, writes=kpkeys)
        if L1CUT < 2:
            return
        dma("pool", a2[:, WB:WB + 8192].rearrange("p (c m) -> p c m", c=8),
            w2d(W["L1_attn_Wqkv"], 0, DM, 2 * DM, 3 * DM), writes=["wqkv"])
        o("pool", lambda e: e.memset(bass.AP(A3, 1024, [[NT * 1040, 128], [1025, NT], [1, 1]]), 1.0),
          writes=["vsb"])
        for t in range(NT):
            for nb in range(2):
                pb_ = 2 + (t * 2 + nb) % 2
                pq_ = psv(pb_)
                for c in range(NCH):
                    o("pe", lambda e, t=t, nb=nb, c=c, pq_=pq_: e.matmul(
                        pq_, a2[:, c * S + t * 128:c * S + t * 128 + 128],
                        a2[:, WB + c * 1024 + nb * 512:WB + c * 1024 + nb * 512 + 512],
                        start=(c == 0), stop=False),
                      reads=["wqkv", "xnt"], writes=pk(pb_))
                o("pe", lambda e, nb=nb, pq_=pq_: e.matmul(
                    pq_, ONEF[0:1, 0:128], TMP[0:1, 2560 + nb * 512:2560 + nb * 512 + 512],
                    start=False, stop=True),
                  reads=tk(2560, 3584) + ["onef"], writes=pk(pb_)),
                o("dve", lambda e, t=t, nb=nb, pq_=pq_: e.tensor_copy(
                    a3[:, t * 1025 + nb * 512:t * 1025 + nb * 512 + 512], pq_),
                  reads=pk(pb_), writes=["vsb"])
        ctx.barrier()
        if L1CUT < 3:
            return
        for dc in range(8):
            for nb in range(2):
                pb_ = (dc % 2) * 2 + nb
                pq_ = psv(pb_)
                for t in range(NT):
                    o("pe", lambda e, dc=dc, nb=nb, t=t, pq_=pq_: e.matmul(
                        pq_, xab[:, t * 1024 + dc * 128:t * 1024 + dc * 128 + 128],
                        a3[:, t * 1025 + nb * 512:t * 1025 + nb * 512 + 512],
                        start=(t == 0), stop=(t == NT - 1)),
                      reads=["kp", "vsb"], writes=pk(pb_))
                o("dve", lambda e, dc=dc, nb=nb, pq_=pq_: e.tensor_copy(
                    a2[:, KVO + dc * 1024 + nb * 512:KVO + dc * 1024 + nb * 512 + 512], pq_),
                  reads=pk(pb_), writes=["kv"])
            for t in range(NT):
                o("pe", lambda e, dc=dc, t=t: e.matmul(
                    ps[:, 4 * 512 + dc:4 * 512 + dc + 1],
                    xab[:, t * 1024 + dc * 128:t * 1024 + dc * 128 + 128],
                    a3[:, t * 1025 + 1024:t * 1025 + 1025],
                    start=(t == 0), stop=(t == NT - 1), skip_group_check=True),
                  reads=["kp", "vsb"], writes=pk(4))
        o("dve", lambda e: e.tensor_copy(a3[:, 16400:16408], ps[:, 4 * 512:4 * 512 + 8]),
          reads=pk(4), writes=["ksum"])
        ctx.barrier()
        if L1CUT < 4:
            return
        for t in range(NT):
            dma("sq", TMP[:, 1024:2048], xs[:, t * DM:(t + 1) * DM],
                writes=tk(1024, 2048), ch="xl")
            for nb in range(2):
                pb_ = (t % 2) * 2 + nb
                pq_ = psv(pb_)
                for c in range(NCH):
                    o("pe", lambda e, t=t, nb=nb, c=c, pq_=pq_: e.matmul(
                        pq_, a2[:, QPT + c * 2048 + t * 128:QPT + c * 2048 + t * 128 + 128],
                        a2[:, KVO + c * 1024 + nb * 512:KVO + c * 1024 + nb * 512 + 512],
                        start=(c == 0), stop=(c == NCH - 1)),
                      reads=["qpt", "kv"], writes=pk(pb_))
            for c in range(NCH):
                o("pe", lambda e, t=t, c=c: e.matmul(
                    ps[:, 5 * 512 + t % 2:5 * 512 + t % 2 + 1],
                    a2[:, QPT + c * 2048 + t * 128:QPT + c * 2048 + t * 128 + 128],
                    a3[:, 16400 + c:16400 + c + 1],
                    start=(c == 0), stop=(c == NCH - 1), skip_group_check=True),
                  reads=["qpt", "ksum"], writes=pk(5))
            o("dve", lambda e, t=t: e.tensor_scalar(
                SMALL[:, 44:45], ps[:, 5 * 512 + t % 2:5 * 512 + t % 2 + 1],
                1e-6, None, OP.add),
              reads=pk(5), writes=["deneps"])
            o("dve", lambda e: e.reciprocal(SMALL[:, 45:46], SMALL[:, 44:45]),
              reads=["deneps"], writes=["rden1"])
            for nb in range(2):
                pb_ = (t % 2) * 2 + nb
                o("dve", lambda e, t=t, nb=nb, pb_=pb_: e.scalar_tensor_tensor(
                    xa[:, t * DM + nb * 512:t * DM + nb * 512 + 512], psv(pb_),
                    SMALL[:, 45:46], TMP[:, 1024 + nb * 512:1024 + nb * 512 + 512],
                    OP.mult, OP.add),
                  reads=pk(pb_) + ["rden1", "kp", "kv"] + tk(1024, 2048), writes=["xa%d" % t])
        ctx.barrier()
        if L1CUT < 5:
            return
        grep = a2[:, FTO:FTO + 2048].bitcast(F32)
        brep = a2[:, FTO + 2048:FTO + 4096].bitcast(F32)

        def load_rep():
            for n in range(2):
                pr_ = psv(6 + n)
                o("pe", lambda e, n=n, pr_=pr_: e.matmul(
                    pr_, ONEF[0:1, 0:128], TMP[0:1, n * 512:n * 512 + 512],
                    start=True, stop=True),
                  reads=tk(0, 1024) + ["onef"], writes=pk(6 + n))
                o("dve", lambda e, n=n, pr_=pr_: e.tensor_copy(
                    grep[:, n * 512:n * 512 + 512], pr_), reads=pk(6 + n), writes=["grep"])
            for n in range(2):
                pr_ = psv(6 + n)
                o("pe", lambda e, n=n, pr_=pr_: e.matmul(
                    pr_, ONEF[0:1, 0:128], TMP[0:1, 1024 + n * 512:1024 + n * 512 + 512],
                    start=True, stop=True),
                  reads=tk(1024, 2048) + ["onef"], writes=pk(6 + n))
                o("dve", lambda e, n=n, pr_=pr_: e.tensor_copy(
                    brep[:, n * 512:n * 512 + 512], pr_), reads=pk(6 + n), writes=["brep"])

        def ln_stats(t, kx, joff):
            sumc, sqc = SMALL[:, 0:1], SMALL[:, 1:2]
            junkD = tmp[:, joff:joff + 512].bitcast(BF16)
            junkA = tmp[:, joff + 512:joff + 1024].bitcast(BF16)
            xt = xa[:, t * DM:(t + 1) * DM]
            o("dve", lambda e, xt=xt, junkD=junkD: e.tensor_scalar(
                junkD, xt, 1.0, 0.0, OP.mult, OP.add, accum_out=sumc),
              reads=[kx], writes=["ln_sum"] + tk(joff, joff + 512))
            o("act", lambda e, xt=xt, junkA=junkA: e.activation(
                junkA, xt, AF.Square, accum_out=sqc),
              reads=[kx], writes=["ln_sq"] + tk(joff + 512, joff + 1024))
            mu, t2, mu2, ve = SMALL[:, 2:3], SMALL[:, 3:4], SMALL[:, 4:5], SMALL[:, 5:6]
            sd, rstd = SMALL[:, 6:7], SMALL[:, 7:8]
            o("dve", lambda e: e.tensor_scalar(mu, sumc, 1.0 / DM, None, OP.mult),
              reads=["ln_sum"], writes=["ln_mu"])
            o("dve", lambda e: e.tensor_scalar(t2, sqc, 1.0 / DM, 1e-5, OP.mult, OP.add),
              reads=["ln_sq"], writes=["ln_t2"])
            o("dve", lambda e: e.tensor_mul(mu2, mu, mu), reads=["ln_mu"], writes=["ln_mu2"])
            o("dve", lambda e: e.tensor_sub(ve, t2, mu2), reads=["ln_t2", "ln_mu2"], writes=["ln_ve"])
            o("act", lambda e: e.activation(sd, ve, AF.Sqrt), reads=["ln_ve"], writes=["ln_sd"])
            o("dve", lambda e: e.reciprocal(rstd, sd), reads=["ln_sd"], writes=["ln_rstd"])
            return mu, rstd

        dma("sq", TMP[0:1, 0:1024], W["L1_attn_g1"][None, :], writes=tk(0, 1024), ch="brows")
        dma("sq", TMP[0:1, 1024:2048], W["L1_attn_b1"][None, :], writes=tk(1024, 2048), ch="brows")
        load_rep()
        for t in range(NT):
            kx = "xa%d" % t
            mu, rstd = ln_stats(t, kx, 1024)
            tt = TMP[:, 2048:3072]
            o("dve", lambda e, t=t, mu=mu, rstd=rstd, tt=tt: e.tensor_scalar(
                tt, xa[:, t * DM:(t + 1) * DM], mu, rstd, OP.subtract, OP.mult),
              reads=[kx, "ln_mu", "ln_rstd"], writes=tk(2048, 3072))
            o("dve", lambda e, tt=tt: e.tensor_mul(tt, tt, grep),
              reads=tk(2048, 3072) + ["grep"], writes=tk(2048, 3072))
            o("dve", lambda e, t=t, tt=tt: e.tensor_add(xa[:, t * DM:(t + 1) * DM], tt, brep),
              reads=tk(2048, 3072) + ["brep"], writes=[kx])
            xnb = tmp[:, 0:512].bitcast(BF16)
            o("act", lambda e, t=t, xnb=xnb: e.activation(xnb, xa[:, t * DM:(t + 1) * DM], AF.Copy),
              reads=[kx], writes=tk(0, 1024))
            for half in range(2):
                pb_ = 6 + half
                pview = psv(pb_, 512)
                for j in range(4):
                    c = half * 4 + j
                    o("pe", lambda e, c=c, j=j, pview=pview, xnb=xnb: e.matmul(
                        pview[:, j * 128:(j + 1) * 128], xnb[:, c * 128:(c + 1) * 128],
                        IDB[:, :], start=(j == 0), stop=(j == 3), skip_group_check=True),
                      reads=tk(0, 1024) + ["idb"], writes=pk(pb_))
                def evac(e, half=half, t=t, pview=pview):
                    d = bass.AP(A2, (half * 4) * S + t * 128, [[A2C, 128], [S, 4], [1, 128]])
                    return e.tensor_copy(d, pview[:, :])
                o("dve", evac, reads=pk(pb_), writes=["x2t"])
        ctx.barrier()
        if L1CUT < 6:
            return
        NHID = 22

        def hsz(h):
            return 42 if h == NHID - 1 else 128

        def rag_cols(src, col0, key):
            cols_from(src, 22, col0, key, ragged_tail=42)

        rag_cols(W["L1_attn_bg"][:], 96, "bgc")
        rag_cols(W["L1_attn_bu"][:], 118, "buc")
        dma("sq", TMP[0:1, 2560:3584], W["L1_attn_bd"][None, :], writes=tk(2560, 3584), ch="brows")
        sb16 = tmp[:, 0:256].bitcast(BF16)
        for sh in (0, 1024):
            for h in range(NHID):
                hs = hsz(h)
                if hs == 128:
                    gsrc = w2d(W["L1_attn_Wg"], 0, DM, h * 128, h * 128 + hs)
                    usrc = w2d(W["L1_attn_Wu"], 0, DM, h * 128, h * 128 + hs)
                else:
                    gsrc = W["L1_attn_Wg"][:, h * 128:h * 128 + hs].rearrange(
                        "(c p) m -> p c m", p=128)
                    usrc = W["L1_attn_Wu"][:, h * 128:h * 128 + hs].rearrange(
                        "(c p) m -> p c m", p=128)
                dma("pool", a2[:, WGB0:WGB0 + 8 * hs].rearrange("p (c m) -> p c m", c=8),
                    gsrc, writes=["slot0"])
                dma("pool", a2[:, WGB1:WGB1 + 8 * hs].rearrange("p (c m) -> p c m", c=8),
                    usrc, writes=["slot1"])
                for nb in range(2):
                    pg_ = psv((h % 2) * 2)
                    pu_ = psv(4 + (h % 2) * 2)
                    for c in range(NCH):
                        o("pe", lambda e, hs=hs, c=c, nb=nb, sh=sh, pg_=pg_: e.matmul(
                            pg_[0:hs, 0:512], a2[:, WGB0 + c * hs:WGB0 + c * hs + hs],
                            a2[:, c * S + sh + nb * 512:c * S + sh + nb * 512 + 512],
                            start=(c == 0), stop=(c == NCH - 1), skip_group_check=True),
                          reads=["slot0", "x2t"], writes=pk((h % 2) * 2))
                    for c in range(NCH):
                        o("pe", lambda e, hs=hs, c=c, nb=nb, sh=sh, pu_=pu_: e.matmul(
                            pu_[0:hs, 0:512], a2[:, WGB1 + c * hs:WGB1 + c * hs + hs],
                            a2[:, c * S + sh + nb * 512:c * S + sh + nb * 512 + 512],
                            start=(c == 0), stop=(c == NCH - 1), skip_group_check=True),
                          reads=["slot1", "x2t"], writes=pk(4 + (h % 2) * 2))
                    o("act", lambda e, hs=hs, h=h, pg_=pg_: e.activation(
                        sb16[0:hs, :], pg_[0:hs, 0:512], AF.Silu,
                        bias=BCOL[0:hs, 96 + h:96 + h + 1]),
                      reads=pk((h % 2) * 2) + ["bgc"], writes=tk(0, 256))
                    o("dve", lambda e, hs=hs, h=h, nb=nb, pu_=pu_: e.scalar_tensor_tensor(
                        a2[0:hs, FTO + h * 1024 + nb * 512:FTO + h * 1024 + nb * 512 + 512],
                        pu_[0:hs, 0:512], BCOL[0:hs, 118 + h:118 + h + 1], sb16[0:hs, :],
                        OP.add, OP.mult),
                      reads=pk(4 + (h % 2) * 2) + ["buc"] + tk(0, 256), writes=["ft"])
            for q4 in range(2):
                s0 = sh + q4 * 512
                for h in range(NHID):
                    hs = hsz(h)
                    wsl = WGB0 if h % 2 == 0 else WGB1
                    if hs == 128:
                        dsrc = w2d(W["L1_attn_Wd"], h * 128, h * 128 + hs, 0, DM)
                    else:
                        dsrc = W["L1_attn_Wd"][h * 128:h * 128 + hs, :].rearrange(
                            "(c p) m -> p c m", p=42)
                    dma("pool", a2[0:hs, wsl:wsl + 1024].rearrange("p (c m) -> p c m", c=1),
                        dsrc, writes=["slot%d" % (h % 2)])
                    for m in range(4):
                        for n in range(2):
                            pb_ = m * 2 + n
                            o("pe", lambda e, h=h, hs=hs, m=m, n=n, q4=q4, wsl=wsl, pb_=pb_: e.matmul(
                                psv(pb_),
                                a2[0:hs, FTO + h * 1024 + q4 * 512 + m * 128:
                                   FTO + h * 1024 + q4 * 512 + m * 128 + 128],
                                a2[0:hs, wsl + n * 512:wsl + n * 512 + 512],
                                start=(h == 0), stop=False, skip_group_check=True),
                              reads=["ft", "slot%d" % (h % 2)], writes=pk(pb_))
                for m in range(4):
                    for n in range(2):
                        pb_ = m * 2 + n
                        o("pe", lambda e, n=n, pb_=pb_: e.matmul(
                            psv(pb_), ONEF[0:1, 0:128],
                            TMP[0:1, 2560 + n * 512:2560 + n * 512 + 512],
                            start=False, stop=True, skip_group_check=True),
                          reads=tk(2560, 3584) + ["onef"], writes=pk(pb_))
                for m in range(4):
                    t_t = s0 // 128 + m
                    for n in range(2):
                        pb_ = m * 2 + n
                        o("dve", lambda e, t_t=t_t, n=n, pb_=pb_: e.tensor_tensor(
                            xa[:, t_t * DM + n * 512:t_t * DM + n * 512 + 512],
                            xa[:, t_t * DM + n * 512:t_t * DM + n * 512 + 512],
                            psv(pb_), OP.add),
                          reads=pk(pb_) + ["xa%d" % t_t], writes=["xa%d" % t_t])
        ctx.barrier()
        if L1CUT < 8:
            return
        dma("sq", TMP[0:1, 0:1024], W["L1_attn_g2"][None, :], writes=tk(0, 1024), ch="brows")
        dma("sq", TMP[0:1, 1024:2048], W["L1_attn_b2"][None, :], writes=tk(1024, 2048), ch="brows")
        load_rep()
        for t in range(NT):
            kx = "xa%d" % t
            mu, rstd = ln_stats(t, kx, 0)
            tt = TMP[:, 2048:3072]
            dma("sq", TMP[:, 1024:2048], xs[:, t * DM:(t + 1) * DM],
                writes=tk(1024, 2048), ch="xl")
            o("dve", lambda e, t=t, mu=mu, rstd=rstd, tt=tt: e.tensor_scalar(
                tt, xa[:, t * DM:(t + 1) * DM], mu, rstd, OP.subtract, OP.mult),
              reads=[kx, "ln_mu", "ln_rstd"], writes=tk(2048, 3072))
            o("dve", lambda e, tt=tt: e.tensor_mul(tt, tt, grep),
              reads=tk(2048, 3072) + ["grep"], writes=tk(2048, 3072))
            o("dve", lambda e, tt=tt: e.tensor_add(tt, tt, brep),
              reads=tk(2048, 3072) + ["brep"], writes=tk(2048, 3072))
            o("dve", lambda e, t=t, tt=tt: e.tensor_add(
                xa[:, t * DM:(t + 1) * DM], tt, TMP[:, 1024:2048]),
              reads=tk(2048, 3072) + tk(1024, 2048), writes=[kx])
        ctx.barrier()

    def emit_store():
        ctx.barrier()
        dma("sq", y_out.rearrange("(t p) d -> p t d", p=128),
            xa.rearrange("p (t d) -> p t d", t=NT), reads=XKEYS, writes=["yout"])
        o("pool", lambda e: e.memset(SMALL[0:1, 63:64], 0.0), reads=["yout"], writes=["fin"])

    setup()
    ctx.barrier()
    subs = [emit_moe_l0, emit_attn_l0, emit_moe_l0, emit_moe_l1, emit_attn_l1, emit_moe_l1]
    for i in range(min(nsub, len(subs))):
        subs[i]()
        ctx.barrier()
    emit_store()
    keys = set()
    for e in Ctx.ENGS:
        keys.add(e)
        for c in ctx.cmds[e]:
            keys.add(c.skey)
    for key in sorted(keys):
        cm = nc.semaphore("sem_" + key); sems[key] = cm.__enter__(); cms.append(cm)
    ctx.finalize(sems)
    for cm in reversed(cms):
        cm.__exit__(None, None, None)
    return nc



def make_in_maps(inputs):
    x = np.ascontiguousarray(np.asarray(inputs["L0_attn_x"], dtype=np.float32))
    base = {}
    for k, v in inputs.items():
        if k == "L0_attn_x":
            continue
        a = np.ascontiguousarray(np.asarray(v, dtype=np.float32))
        if a.ndim == 0:
            a = a.reshape(1, 1)
        base[k] = a
    return [dict(base, x=np.ascontiguousarray(x[c])) for c in range(8)]


def kernel(**inputs):
    nc = build_program(N_SUB)
    in_maps = make_in_maps(inputs)
    res = run_bass_kernel_spmd(nc, in_maps, core_ids=list(range(8)))
    return np.stack([r["y"] for r in res.results], axis=0).astype(np.float32)


# version 05
# speedup vs baseline: 1.1207x; 1.1207x over previous
import os
import numpy as np
import concourse.bass as bass
import concourse.mybir as mybir
from concourse.bass_utils import run_bass_kernel_spmd
from concourse.alu_op_type import AluOpType as OP

F32 = mybir.dt.float32
BF16 = mybir.dt.bfloat16
AF = mybir.ActivationFunctionType
AX = mybir.AxisListType.X

S = 2048
DM = 1024
NT = S // 128
NCH = DM // 128
N_SUB = 6
L1CUT = int(os.environ.get("L1CUT", "9"))


class Cmd:
    __slots__ = ("fn", "waits", "skey", "delta", "inc_val")
    def __init__(self, fn, waits, skey, delta):
        self.fn = fn; self.waits = waits; self.skey = skey; self.delta = delta
        self.inc_val = None


class Ctx:
    ENGS = ("sq", "pool", "pe", "dve", "act")

    def __init__(self, nc):
        self.nc = nc
        self.cmds = {e: [] for e in self.ENGS}
        self.attached = {e: [] for e in self.ENGS}
        self.cum = {}
        self.known = {}
        self.bufs = {}
        self.pending = {e: {} for e in self.ENGS}

    def _event(self, eng, idx):
        arr = self.attached[eng]
        lo, hi = 0, len(arr)
        while lo < hi:
            mid = (lo + hi) // 2
            if arr[mid][0] < idx: lo = mid + 1
            else: hi = mid
        if lo < len(arr):
            return arr[lo][1], arr[lo][2]
        c = self.cmds[eng][idx]
        v = self.cum.get(c.skey, 0) + c.delta
        self.cum[c.skey] = v
        c.inc_val = v
        arr.append((idx, c.skey, v))
        return c.skey, v

    def op(self, eng, fn, reads=(), writes=(), dma=False):
        deps = {}
        def dep(se, si):
            if se == eng == "pe":
                return
            key, v = self._event(se, si)
            if self.known.get((eng, key), 0) >= v: return
            if deps.get(key, 0) < v: deps[key] = v
        for b in reads:
            st = self.bufs.get(b)
            if st and st["w"]: dep(*st["w"])
        for b in writes:
            st = self.bufs.get(b)
            if st:
                if st["w"]: dep(*st["w"])
                for se, si in st["r"].items(): dep(se, si)
        for key, v in self.pending[eng].items():
            if self.known.get((eng, key), 0) < v and deps.get(key, 0) < v: deps[key] = v
        self.pending[eng] = {}
        idx = len(self.cmds[eng])
        for b in writes:
            self.bufs[b] = {"w": (eng, idx), "r": {}}
        for b in reads:
            st = self.bufs.setdefault(b, {"w": None, "r": {}})
            st["r"][eng] = idx
        for key, v in deps.items():
            self.known[(eng, key)] = v
        if dma:
            skey, delta = eng + "_d_" + (dma if isinstance(dma, str) else "x"), 16
        else:
            skey, delta = eng, 1
        self.cmds[eng].append(Cmd(fn, sorted(deps.items()), skey, delta))
        if dma:
            self._event(eng, idx)
        return (eng, idx)

    def dma(self, q, dst, src, reads=(), writes=(), ch=None):
        c = ch or (writes[0] if writes else "x")
        wl = list(writes) + ["__ch_%s_%s" % (q, c)]
        if q == "pool":
            n = self.cum.get("__pooldma_n", 0)
            self.cum["__pooldma_n"] = n + 1
            wl.append("__pool_rr%d" % (n % 2))
        return self.op(q, lambda eng, d=dst, s=src: eng.dma_start(d, s),
                       reads=reads, writes=wl, dma=c)

    def barrier(self):
        evs = []
        for e in self.ENGS:
            n = len(self.cmds[e])
            if n: evs.append((e, self._event(e, n - 1)))
        for e in self.ENGS:
            for se, (key, v) in evs:
                if se == e == "pe": continue
                if self.known.get((e, key), 0) < v:
                    if self.pending[e].get(key, 0) < v: self.pending[e][key] = v

    def finalize(self, sems):
        with self.nc.Block() as block:
            def mk(ename):
                def body(eng):
                    for c in self.cmds[ename]:
                        for key, v in c.waits:
                            eng.wait_ge(sems[key], v)
                        inst = c.fn(eng)
                        if c.inc_val is not None:
                            inst.then_inc(sems[c.skey], c.delta)
                return body
            block.sync(mk("sq"))
            block.gpsimd(mk("pool"))
            block.tensor(mk("pe"))
            block.vector(mk("dve"))
            block.scalar(mk("act"))



def build_program(nsub=N_SUB):
    nc = bass.Bass("TRN2", target_bir_lowering=False)
    ctx = Ctx(nc)
    o, dma = ctx.op, ctx.dma

    def din(name, shape):
        return nc.dram_tensor(name, shape, F32, kind="ExternalInput").ap()

    x_in = din("x", [S, DM])
    y_out = nc.dram_tensor("y", [S, DM], F32, kind="ExternalOutput").ap()
    W = {}
    for name, shape in [
        ("L0_attn_Wq", [DM, DM]), ("L0_attn_Wkv_short", [DM, 2 * DM]),
        ("L0_attn_Wkv_long", [DM, 2 * DM]), ("L0_attn_Wo", [DM, DM]),
        ("L0_attn_mix_weight", [1, 1]), ("L0_attn_decay_factor", [1, 1]),
        ("L0_ffn_Wr", [DM, 8]), ("L0_ffn_W1", [8, DM, 2048]),
        ("L0_ffn_W2", [8, 2048, DM]), ("L0_ffn_W3", [8, DM, 2048]),
        ("L1_attn_Wqkv", [DM, 3 * DM]), ("L1_attn_bqkv", [3 * DM]),
        ("L1_attn_Wg", [DM, 2730]), ("L1_attn_bg", [2730]),
        ("L1_attn_Wu", [DM, 2730]), ("L1_attn_bu", [2730]),
        ("L1_attn_Wd", [2730, DM]), ("L1_attn_bd", [DM]),
        ("L1_attn_g1", [DM]), ("L1_attn_b1", [DM]),
        ("L1_attn_g2", [DM]), ("L1_attn_b2", [DM]),
        ("L1_ffn_Wr1", [DM, 256]), ("L1_ffn_br1", [256]),
        ("L1_ffn_Wr2", [256, 8]), ("L1_ffn_br2", [8]),
        ("L1_ffn_Ws1", [DM, 4096]), ("L1_ffn_bs1", [4096]),
        ("L1_ffn_Ws2", [4096, DM]), ("L1_ffn_bs2", [DM]),
        ("L1_ffn_We1", [8, DM, 4096]), ("L1_ffn_be1", [8, 4096]),
        ("L1_ffn_We2", [8, 4096, DM]), ("L1_ffn_be2", [8, DM]),
    ]:
        W[name] = din(name, shape)

    xs = nc.dram_tensor("xscr", [128, NT * DM], F32).ap()
    ktl_s = nc.dram_tensor("ktlscr", [128, NCH * S], BF16).ap()
    vl_s = nc.dram_tensor("vlscr", [128, NT * 1040], BF16).ap()

    cms = []
    def sb(name, shape, dt):
        cm = nc.sbuf_tensor(name, shape, dt); t = cm.__enter__(); cms.append(cm); return t

    A2C = 43008
    XA = sb("XA", [128, NT * DM], F32)
    A2 = sb("A2", [128, A2C], BF16)
    A3 = sb("A3", [128, NT * 1040], BF16)
    TMP = sb("TMP", [128, 3840], F32)
    IDB = sb("IDB", [128, 128], BF16)
    IDF = sb("IDF", [128, 128], F32)
    IOP = sb("IOP", [128, 1], F32)
    IOF = sb("IOF", [128, 1024], F32)
    ONEF = sb("ONEF", [1, 128], F32)
    BCOL = sb("BCOL", [128, 160], F32)
    BMAT = sb("BMAT", [16, 1024], F32)
    GAT = sb("GAT", [128, NT * 10], F32)
    SMALL = sb("SMALL", [128, 64], F32)
    EMSK = sb("EMSK", [128, 128], BF16)
    PScm = nc.psum_tensor("PS", [128, 4096], F32)
    PSh = PScm.__enter__(); cms.append(PScm)

    sems = {}

    xa, a2, a3, tmp, ps = XA.ap(), A2.ap(), A3.ap(), TMP.ap(), PSh.ap()
    xab = xa.bitcast(BF16)

    def psv(bank, cols=512):
        return ps[:, bank * 512:bank * 512 + cols]
    def pk(bank, cols=512):
        return ["ps%d" % b for b in range(bank, bank + (cols + 511) // 512)]

    XKEYS = ["xa%d" % t for t in range(NT)]

    def ldw_bf(dst_off, ncols_chunks, chunk_cols, src_ap, key):
        dst = a2[:, dst_off:dst_off + ncols_chunks * chunk_cols].rearrange(
            "p (c m) -> p c m", c=ncols_chunks)
        dma("pool", dst, src_ap, writes=[key])

    def w2d(wap, r0, r1, c0, c1):
        return wap[r0:r1, c0:c1].rearrange("(c p) m -> p c m", p=128)

    def w3d(wap, e_i, r0, r1, c0, c1):
        return wap[e_i:e_i + 1, r0:r1, c0:c1].rearrange("a (c p) m -> p (a c) m", p=128)

    def bias_cols(src_row_ap, n, col0, keyr, keyc, zero_tail=False):
        m = (n + 127) // 128
        rows = TMP[0:m, 0:128]
        if zero_tail:
            o("pool", lambda e: e.memset(TMP[max(0, m - 1):m, 0:128], 0.0), writes=[keyr])
        full = (n // 128) * 128
        if full:
            dma("sq", TMP[0:n // 128, 0:128],
                src_row_ap[0:full].rearrange("(r p) -> r p", p=128),
                writes=[keyr])
        if n % 128:
            dma("sq", TMP[n // 128:n // 128 + 1, 0:n % 128], src_row_ap[None, full:n],
                writes=[keyr])
        pt = psv(7, 128)
        o("pe", lambda e: e.matmul(pt[:, 0:m], rows, IDF[0:m, 0:m], start=True, stop=True),
          reads=[keyr, "idf"], writes=pk(7))
        o("dve", lambda e: e.tensor_copy(BCOL[:, col0:col0 + m], pt[:, 0:m]),
          reads=pk(7), writes=[keyc])

    def transpose_128(src_ap, dst_base_off, dst_stride, n_sub, rkeys, wkey, dst_sub0=0):
        for g0 in range(0, n_sub, 4):
            gn = min(4, n_sub - g0)
            pb = 6 + (g0 // 4) % 2
            pview = psv(pb, 512)
            for j in range(gn):
                o("pe", lambda e, j=j, g0=g0, pview=pview: e.matmul(
                    pview[:, j * 128:(j + 1) * 128],
                    src_ap[:, (g0 + j) * 128:(g0 + j + 1) * 128], IDB[:, :],
                    start=(j == 0), stop=(j == gn - 1), skip_group_check=True),
                  reads=rkeys + ["idb"], writes=pk(pb))
            def evac(e, g0=g0, gn=gn, pview=pview):
                d = a2[:, dst_base_off + (dst_sub0 + g0) * dst_stride:
                       dst_base_off + (dst_sub0 + g0 + (gn - 1)) * dst_stride + 128]
                d = bass.AP(A2, dst_base_off + (dst_sub0 + g0) * dst_stride,
                            [[A2C, 128], [dst_stride, gn], [1, 128]])
                return e.tensor_copy(d, pview[:, 0:gn * 128])
            o("dve", evac, reads=pk(pb), writes=[wkey])

    def setup():
        o("pool", lambda e: e.iota(IOP[:, :], [[0, 1]], channel_multiplier=1,
                                   allow_small_or_imprecise_dtypes=True), writes=["iop"])
        o("pool", lambda e: e.iota(IOF[:, :], [[1, 1024]], channel_multiplier=0,
                                   allow_small_or_imprecise_dtypes=True), writes=["iof"])
        o("pool", lambda e: e.iota(TMP[:, 0:128], [[1, 128]], channel_multiplier=0,
                                   allow_small_or_imprecise_dtypes=True), writes=["tmpi"])
        o("dve", lambda e: e.tensor_scalar(IDF[:, :], TMP[:, 0:128], IOP[:, 0:1], None,
                                           OP.is_equal), reads=["tmpi", "iop"], writes=["idf"])
        o("dve", lambda e: e.tensor_copy(IDB[:, :], IDF[:, :]), reads=["idf"], writes=["idb"])
        o("pool", lambda e: e.memset(ONEF[:, :], 1.0), writes=["onef"])
        o("pool", lambda e: e.memset(EMSK[:, :], 0.0), writes=["emsk"])
        for c in range(8):
            o("pool", lambda e, c=c: e.memset(EMSK[0:64, 16 * c + 2 * c:16 * c + 2 * c + 1], 1.0),
              writes=["emsk"])
            o("pool", lambda e, c=c: e.memset(EMSK[64:128, 16 * c + 2 * c + 1:16 * c + 2 * c + 2], 1.0),
              writes=["emsk"])
        dma("sq", xa.rearrange("p (t d) -> p t d", t=NT),
            x_in.rearrange("(t p) d -> p t d", p=128), writes=XKEYS)

    def emit_ln_to_xnT(affine=None, src_keys=None, dst_key="xnt"):
        skeys = src_keys or XKEYS
        for t in range(NT):
            xt = xa[:, t * DM:(t + 1) * DM]
            kx = skeys[t]
            sumc, sqc = SMALL[:, 0:1], SMALL[:, 1:2]
            junkD = tmp[:, 1024:1536].bitcast(BF16)
            junkA = tmp[:, 1536:2048].bitcast(BF16)
            o("dve", lambda e, xt=xt: e.tensor_scalar(junkD, xt, 1.0, 0.0, OP.mult, OP.add,
                                                      accum_out=sumc),
              reads=[kx], writes=["ln_sum", "jD"])
            o("act", lambda e, xt=xt: e.activation(junkA, xt, AF.Square, accum_out=sqc),
              reads=[kx], writes=["ln_sq", "jA"])
            mu, t2, mu2, ve = SMALL[:, 2:3], SMALL[:, 3:4], SMALL[:, 4:5], SMALL[:, 5:6]
            sd, rstd = SMALL[:, 6:7], SMALL[:, 7:8]
            o("dve", lambda e: e.tensor_scalar(mu, sumc, 1.0 / DM, None, OP.mult),
              reads=["ln_sum"], writes=["ln_mu"])
            o("dve", lambda e: e.tensor_scalar(t2, sqc, 1.0 / DM, 1e-5, OP.mult, OP.add),
              reads=["ln_sq"], writes=["ln_t2"])
            o("dve", lambda e: e.tensor_mul(mu2, mu, mu), reads=["ln_mu"], writes=["ln_mu2"])
            o("dve", lambda e: e.tensor_sub(ve, t2, mu2), reads=["ln_t2", "ln_mu2"], writes=["ln_ve"])
            o("act", lambda e: e.activation(sd, ve, AF.Sqrt), reads=["ln_ve"], writes=["ln_sd"])
            o("dve", lambda e: e.reciprocal(rstd, sd), reads=["ln_sd"], writes=["ln_rstd"])
            xnb = tmp[:, 0:512].bitcast(BF16)
            o("dve", lambda e, xt=xt: e.tensor_scalar(xnb, xt, mu, rstd, OP.subtract, OP.mult),
              reads=[kx, "ln_mu", "ln_rstd"], writes=["xnb"])
            for half in range(2):
                pb = 6 + half
                pview = psv(pb, 512)
                for j in range(4):
                    c = half * 4 + j
                    o("pe", lambda e, c=c, j=j, pview=pview: e.matmul(
                        pview[:, j * 128:(j + 1) * 128], xnb[:, c * 128:(c + 1) * 128],
                        IDB[:, :], start=(j == 0), stop=(j == 3), skip_group_check=True),
                      reads=["xnb", "idb"], writes=pk(pb))
                def evac(e, half=half, t=t, pview=pview):
                    d = bass.AP(A2, (half * 4) * S + t * 128, [[A2C, 128], [S, 4], [1, 128]])
                    return e.tensor_copy(d, pview[:, :])
                o("dve", evac, reads=pk(pb), writes=[dst_key])

    def xnt(c, s0, n):
        return a2[:, c * S + s0:c * S + s0 + n]

    def emit_moe_l0():
        emit_ln_to_xnT()
        ctx.barrier()
        WRB = 40960
        W1O, W3O, W2O = 16384, 24576, 32768
        def ldW13(e_i, qg):
            for nm_, off, wt in (("w1", W1O, W["L0_ffn_W1"]), ("w3", W3O, W["L0_ffn_W3"])):
                dma("pool", a2[:, off:off + 8192].rearrange("p (c m) -> p c m", c=8),
                    w3d(wt, e_i, 0, DM, qg * 1024, (qg + 1) * 1024), writes=[nm_])
        def ldW2(e_i, qg):
            dma("pool", a2[:, W2O:W2O + 8192].rearrange("p (c m) -> p c m", c=8),
                w3d(W["L0_ffn_W2"], e_i, qg * 1024, (qg + 1) * 1024, 0, DM), writes=["w2"])
        dma("pool", a2[:, WRB:WRB + 64].rearrange("p (c m) -> p c m", c=8),
            w2d(W["L0_ffn_Wr"], 0, DM, 0, 8), writes=["wrb"])
        ldW13(0, 0); ldW2(0, 0)
        for t in range(NT):
            pg = psv(0, 8)
            for c in range(NCH):
                o("pe", lambda e, t=t, c=c, pg=pg: e.matmul(
                    pg, xnt(c, t * 128, 128), a2[:, WRB + c * 8:WRB + c * 8 + 8],
                    start=(c == 0), stop=(c == NCH - 1)),
                  reads=["xnt", "wrb"], writes=pk(0))
            nm, sm, rs = SMALL[:, 8:9], SMALL[:, 9:10], SMALL[:, 10:11]
            m1, m2, dn, rd = SMALL[:, 11:12], SMALL[:, 12:13], SMALL[:, 13:14], SMALL[:, 14:15]
            pu, pr = SMALL[:, 16:24], SMALL[:, 24:32]
            eq1, tq, ge2, t1 = SMALL[:, 32:40], SMALL[:, 40:48], SMALL[:, 48:56], SMALL[:, 56:64]
            o("dve", lambda e, pg=pg: e.reduce_max(nm, pg, axis=AX, negate=True),
              reads=pk(0), writes=["g_nm"])
            o("act", lambda e, pg=pg: e.activation(pu, pg, AF.Exp, bias=nm),
              reads=pk(0) + ["g_nm"], writes=["g_pu"])
            o("dve", lambda e: e.reduce_sum(sm, pu, axis=AX), reads=["g_pu"], writes=["g_sm"])
            o("dve", lambda e: e.reciprocal(rs, sm), reads=["g_sm"], writes=["g_rs"])
            o("dve", lambda e: e.tensor_scalar(pr, pu, rs, None, OP.mult),
              reads=["g_pu", "g_rs"], writes=["g_pr"])
            o("dve", lambda e: e.reduce_max(m1, pr, axis=AX), reads=["g_pr"], writes=["g_m1"])
            o("dve", lambda e: e.tensor_scalar(eq1, pr, m1, 2.0, OP.is_equal, OP.mult),
              reads=["g_pr", "g_m1"], writes=["g_eq1"])
            o("dve", lambda e: e.tensor_sub(tq, pr, eq1), reads=["g_pr", "g_eq1"], writes=["g_tq"])
            o("dve", lambda e: e.reduce_max(m2, tq, axis=AX), reads=["g_tq"], writes=["g_m2"])
            o("dve", lambda e: e.tensor_scalar(ge2, pr, m2, None, OP.is_ge),
              reads=["g_pr", "g_m2"], writes=["g_ge2"])
            o("dve", lambda e: e.tensor_add(dn, m1, m2), reads=["g_m1", "g_m2"], writes=["g_dn"])
            o("dve", lambda e: e.reciprocal(rd, dn), reads=["g_dn"], writes=["g_rd"])
            o("dve", lambda e: e.tensor_scalar(t1, pr, rd, 0.5, OP.mult, OP.mult),
              reads=["g_pr", "g_rd"], writes=["g_t1"])
            o("dve", lambda e, t=t: e.tensor_mul(GAT[:, t * 10:t * 10 + 8], t1, ge2),
              reads=["g_t1", "g_ge2"], writes=["gat"])
        ctx.barrier()
        for nset in range(16):
            e_i, qg = nset // 2, nset % 2
            nxt = (nset + 1) // 2, (nset + 1) % 2
            for sb in range(4):
                s0 = sb * 512
                for h in range(8):
                    bs_ = h % 2
                    p1, p3 = psv(0 + bs_), psv(2 + bs_)
                    for c in range(NCH):
                        o("pe", lambda e, c=c, h=h, s0=s0, p1=p1: e.matmul(
                            p1, a2[:, W1O + c * 1024 + h * 128:W1O + c * 1024 + h * 128 + 128],
                            xnt(c, s0, 512), start=(c == 0), stop=(c == NCH - 1)),
                          reads=["w1", "xnt"], writes=pk(0 + bs_))
                    for c in range(NCH):
                        o("pe", lambda e, c=c, h=h, s0=s0, p3=p3: e.matmul(
                            p3, a2[:, W3O + c * 1024 + h * 128:W3O + c * 1024 + h * 128 + 128],
                            xnt(c, s0, 512), start=(c == 0), stop=(c == NCH - 1)),
                          reads=["w3", "xnt"], writes=pk(2 + bs_))
                    t1b = tmp[:, 512 + bs_ * 256:512 + bs_ * 256 + 256].bitcast(BF16)
                    o("act", lambda e, p1=p1, t1b=t1b: e.activation(t1b, p1, AF.Silu),
                      reads=pk(0 + bs_), writes=["t1b%d" % bs_])
                    o("dve", lambda e, p3=p3, t1b=t1b, h=h: e.tensor_tensor(
                        a3[:, h * 512:h * 512 + 512], t1b, p3, OP.mult),
                      reads=["t1b%d" % bs_] + pk(2 + bs_), writes=["hc%d" % h])
                if nset < 15 and sb == 3:
                    ldW13(*nxt)
                for m in range(4):
                    t_t = sb * 4 + m
                    for n in range(2):
                        bsel = 4 + (m * 2 + n) % 2
                        po = psv(bsel)
                        for h in range(8):
                            o("pe", lambda e, h=h, m=m, n=n, po=po: e.matmul(
                                po, a3[:, h * 512 + m * 128:h * 512 + m * 128 + 128],
                                a2[:, W2O + h * 1024 + n * 512:W2O + h * 1024 + n * 512 + 512],
                                start=(h == 0), stop=(h == 7)),
                              reads=["hc%d" % h, "w2"], writes=pk(bsel))
                        o("dve", lambda e, po=po, t_t=t_t, n=n, e_i=e_i: e.scalar_tensor_tensor(
                            xa[:, t_t * DM + n * 512:t_t * DM + n * 512 + 512], po,
                            GAT[:, t_t * 10 + e_i:t_t * 10 + e_i + 1],
                            xa[:, t_t * DM + n * 512:t_t * DM + n * 512 + 512],
                            OP.mult, OP.add),
                          reads=pk(bsel) + ["gat", "xa%d" % t_t], writes=["xa%d" % t_t])
                if nset < 15 and sb == 3:
                    ldW2(*nxt)

    def emit_attn_l0():
        emit_ln_to_xnT()
        dma("sq", xs[:, :], xa[:, :], reads=XKEYS, writes=["xspill"])
        ctx.barrier()
        QTO, KTO = 0, 16384
        WQB, WKB, WVB = 16384, 24576, 32768
        DSL, SLAB, WOO, EXPO = 0, 16384, 24576, 32768
        ACOL, NCCOL = 64, 67
        dma("sq", SMALL[0:1, 0:1], W["L0_attn_mix_weight"][0:1, 0:1], writes=["mixs"])
        dma("sq", SMALL[0:1, 1:2], W["L0_attn_decay_factor"][0:1, 0:1], writes=["decs"])
        o("act", lambda e: e.activation(SMALL[0:1, 2:3], SMALL[0:1, 0:1], AF.Sigmoid),
          reads=["mixs"], writes=["sigm"])
        prep = psv(7, 128)
        o("pe", lambda e: e.matmul(prep[:, 0:1], ONEF[0:1, 0:128], SMALL[0:1, 2:3],
                                   start=True, stop=False, skip_group_check=True),
          reads=["sigm", "onef"], writes=pk(7))
        o("pe", lambda e: e.matmul(prep[:, 1:2], ONEF[0:1, 0:128], SMALL[0:1, 1:2],
                                   start=False, stop=True, skip_group_check=True),
          reads=["decs", "onef"], writes=pk(7))
        o("dve", lambda e: e.tensor_copy(BCOL[:, ACOL:ACOL + 2], prep[:, 0:2]),
          reads=pk(7), writes=["acol"])
        o("dve", lambda e: e.tensor_scalar(BCOL[:, NCCOL + 1:NCCOL + 2],
                                           BCOL[:, ACOL + 1:ACOL + 2], 1.0, None, OP.subtract),
          reads=["acol"], writes=["ncol"])

        def proj_feature_major(wslot, wkey, dsto, dkey):
            for co in range(8):
                for nb in range(4):
                    pb = (co * 4 + nb) % 2
                    pq_ = psv(pb)
                    for c in range(NCH):
                        o("pe", lambda e, co=co, nb=nb, c=c, pq_=pq_, wslot=wslot: e.matmul(
                            pq_, a2[:, wslot + c * 1024 + co * 128:wslot + c * 1024 + co * 128 + 128],
                            a2[:, c * S + nb * 512:c * S + nb * 512 + 512],
                            start=(c == 0), stop=(c == NCH - 1)),
                          reads=[wkey, "xnt"], writes=pk(pb))
                    o("dve", lambda e, co=co, nb=nb, pq_=pq_, dsto=dsto: e.tensor_copy(
                        xab[:, dsto + co * 2048 + nb * 512:dsto + co * 2048 + nb * 512 + 512], pq_),
                      reads=pk(pb), writes=[dkey])

        def proj_v(wslot, wkey):
            o("pool", lambda e: e.memset(bass.AP(A3, 64, [[NT * 1040, 128], [65, 256]]), 1.0),
              writes=["vsb"])
            for t in range(NT):
                for nb in range(2):
                    pb = 2 + (t * 2 + nb) % 2
                    pq_ = psv(pb)
                    for c in range(NCH):
                        o("pe", lambda e, t=t, nb=nb, c=c, pq_=pq_, wslot=wslot: e.matmul(
                            pq_, a2[:, c * S + t * 128:c * S + t * 128 + 128],
                            a2[:, wslot + c * 1024 + nb * 512:wslot + c * 1024 + nb * 512 + 512],
                            start=(c == 0), stop=(c == NCH - 1)),
                          reads=[wkey, "xnt"], writes=pk(pb))
                    def evv(e, t=t, nb=nb, pq_=pq_):
                        d = bass.AP(A3, t * 1040 + nb * 8 * 65, [[NT * 1040, 128], [65, 8], [1, 64]])
                        return e.tensor_copy(d, pq_.rearrange("p (h j) -> p h j", h=8))
                    o("dve", evv, reads=pk(pb), writes=["vsb"])

        def cnorms(srco, srckey, outcol):
            for c in range(NCH):
                sq = tmp[:, 0:1024].bitcast(BF16)
                o("dve", lambda e, c=c, srco=srco: e.tensor_tensor(
                    sq, xab[:, srco + c * 2048:srco + c * 2048 + 2048],
                    xab[:, srco + c * 2048:srco + c * 2048 + 2048], OP.mult),
                  reads=[srckey], writes=["csq"])
                for nb in range(4):
                    o("pe", lambda e, c=c, nb=nb: e.matmul(
                        ps[0:16, nb * 512:nb * 512 + 512], EMSK[:, 16 * c:16 * c + 16],
                        sq[:, nb * 512:nb * 512 + 512],
                        start=(c == 0), stop=(c == NCH - 1), skip_group_check=True),
                      reads=["csq", "emsk"], writes=pk(nb))
            o("dve", lambda e: e.reduce_max(SMALL[0:16, outcol:outcol + 1], ps[0:16, 0:2048],
                                            axis=AX),
              reads=pk(0) + pk(1) + pk(2) + pk(3), writes=["cn%d" % outcol])

        def negc_rep(col_q, col_k, dstcol, key):
            o("dve", lambda e: e.tensor_mul(SMALL[0:16, 34:35], SMALL[0:16, col_q:col_q + 1],
                                            SMALL[0:16, col_k:col_k + 1]),
              reads=["cn%d" % col_q, "cn%d" % col_k], writes=["cprod"])
            o("act", lambda e: e.activation(SMALL[0:16, 35:36], SMALL[0:16, 34:35], AF.Sqrt,
                                            scale=1.0 / 64.0),
              reads=["cprod"], writes=["csqr"])
            o("dve", lambda e: e.tensor_scalar(SMALL[0:16, 36:37], SMALL[0:16, 35:36],
                                               -1.0, None, OP.mult),
              reads=["csqr"], writes=["cneg"])
            pb = psv(6, 128)
            o("pe", lambda e: e.matmul(pb[0:1, 0:16], SMALL[0:16, 36:37], IDF[0:16, 0:16],
                                       start=True, stop=True, skip_group_check=True),
              reads=["cneg", "idf"], writes=pk(6))
            o("dve", lambda e: e.tensor_copy(TMP[0:1, 2048:2064], pb[0:1, 0:16]),
              reads=pk(6), writes=["crow"])
            pb2 = psv(7, 128)
            o("pe", lambda e: e.matmul(pb2[:, 0:16], ONEF[0:1, 0:128], TMP[0:1, 2048:2064],
                                       start=True, stop=True, skip_group_check=True),
              reads=["crow", "onef"], writes=pk(7))
            o("dve", lambda e, dstcol=dstcol: e.tensor_copy(BCOL[:, dstcol:dstcol + 16],
                                                            pb2[:, 0:16]),
              reads=pk(7), writes=[key])

        dma("pool", a2[:, WQB:WQB + 8192].rearrange("p (c m) -> p c m", c=8),
            w2d(W["L0_attn_Wq"], 0, DM, 0, DM), writes=["wqb"])
        proj_feature_major(WQB, "wqb", QTO, "qt")
        cnorms(QTO, "qt", 32)
        dma("pool", a2[:, WKB:WKB + 8192].rearrange("p (c m) -> p c m", c=8),
            w2d(W["L0_attn_Wkv_long"], 0, DM, 0, DM), writes=["wkb"])
        proj_feature_major(WKB, "wkb", KTO, "kt")
        cnorms(KTO, "kt", 33)
        negc_rep(32, 33, 112, "negCL")
        dma("pool", a2[:, WVB:WVB + 8192].rearrange("p (c m) -> p c m", c=8),
            w2d(W["L0_attn_Wkv_long"], 0, DM, DM, 2 * DM), writes=["wvb"])
        proj_v(WVB, "wvb")
        dma("sq", ktl_s[:, :], xab[:, KTO:KTO + 16384], reads=["kt"], writes=["ktspill"])
        dma("sq", vl_s[:, :], a3[:, 0:16640], reads=["vsb"], writes=["vlspill"])
        ctx.barrier()
        dma("pool", a2[:, WKB:WKB + 8192].rearrange("p (c m) -> p c m", c=8),
            w2d(W["L0_attn_Wkv_short"], 0, DM, 0, DM), writes=["wkb"])
        proj_feature_major(WKB, "wkb", KTO, "kt")
        cnorms(KTO, "kt", 33)
        negc_rep(32, 33, 96, "negCS")
        dma("pool", a2[:, WVB:WVB + 8192].rearrange("p (c m) -> p c m", c=8),
            w2d(W["L0_attn_Wkv_short"], 0, DM, DM, 2 * DM), writes=["wvb"])
        proj_v(WVB, "wvb")
        ctx.barrier()
        dma("pool", a2[:, WOO:WOO + 8192].rearrange("p (c m) -> p c m", c=8),
            w2d(W["L0_attn_Wo"], 0, DM, 0, DM), writes=["wo"])

        def attn_phase(is_short):
            ncc = 96 if is_short else 112
            nck = "negCS" if is_short else "negCL"
            for qb in (0, 1024):
                if is_short:
                    for kc in range(NT):
                        d0 = float(kc * 128 - qb)
                        o("dve", lambda e, d0=d0: e.tensor_scalar(SMALL[:, 40:41], IOP[:, 0:1],
                                                                  d0, None, OP.add),
                          reads=["iop"], writes=["pdc"])
                        o("dve", lambda e: e.tensor_scalar(TMP[:, 0:1024], IOF[:, 0:1024],
                                                           SMALL[:, 40:41],
                                                           BCOL[:, NCCOL + 1:NCCOL + 2],
                                                           OP.subtract, OP.mult),
                          reads=["iof", "pdc", "ncol"], writes=["dabs"])
                        o("act", lambda e: e.activation(TMP[:, 0:1024], TMP[:, 0:1024], AF.Abs),
                          reads=["dabs"], writes=["dabs"])
                        o("act", lambda e, kc=kc: e.activation(
                            a2[:, DSL + kc * 1024:DSL + kc * 1024 + 1024], TMP[:, 0:1024],
                            AF.Exp, scale=-1.0),
                          reads=["dabs"], writes=["dsl"])
                for h in range(16):
                    hc, hb = h // 2, 64 * (h % 2)
                    avb = 4 + 2 * (h % 2)
                    for kc in range(NT + 1):
                        if kc < NT:
                            qkb = 2 * (kc % 2)
                            pq_ = psv(qkb, 1024)
                            for nb in range(2):
                                o("pe", lambda e, hc=hc, hb=hb, kc=kc, nb=nb, qb=qb, pq_=pq_: e.matmul(
                                    pq_[:, nb * 512:nb * 512 + 512],
                                    xab[hb:hb + 64, KTO + hc * 2048 + kc * 128:KTO + hc * 2048 + kc * 128 + 128],
                                    xab[hb:hb + 64, QTO + hc * 2048 + qb + nb * 512:QTO + hc * 2048 + qb + nb * 512 + 512],
                                    start=True, stop=True, skip_group_check=True),
                                  reads=["kt", "qt"], writes=pk(qkb, 1024))
                            eslot = EXPO + (kc % 2) * 1024
                            o("act", lambda e, pq_=pq_, eslot=eslot, h=h, ncc=ncc, qkb=qkb: e.activation(
                                a2[:, eslot:eslot + 1024], pq_[:, 0:1024], AF.Exp,
                                bias=BCOL[:, ncc + h:ncc + h + 1], scale=0.125),
                              reads=pk(qkb, 1024) + [nck], writes=["exp%d" % (kc % 2)])
                            if is_short:
                                o("dve", lambda e, eslot=eslot, kc=kc: e.tensor_tensor(
                                    a2[:, eslot:eslot + 1024], a2[:, eslot:eslot + 1024],
                                    a2[:, DSL + kc * 1024:DSL + kc * 1024 + 1024], OP.mult),
                                  reads=["exp%d" % (kc % 2), "dsl"], writes=["exp%d" % (kc % 2)])
                        if kc >= 1:
                            kp_ = kc - 1
                            eslot2 = EXPO + (kp_ % 2) * 1024
                            for m in range(8):
                                o("pe", lambda e, eslot2=eslot2, m=m, kp_=kp_, h=h, avb=avb: e.matmul(
                                    ps[:, (avb + m // 4) * 512 + (m % 4) * 65:
                                       (avb + m // 4) * 512 + (m % 4) * 65 + 65],
                                    a2[:, eslot2 + m * 128:eslot2 + m * 128 + 128],
                                    a3[:, kp_ * 1040 + 65 * h:kp_ * 1040 + 65 * h + 65],
                                    start=(kp_ == 0), stop=(kp_ == NT - 1), skip_group_check=True),
                                  reads=["exp%d" % (kp_ % 2), "vsb"], writes=pk(avb + m // 4))
                    for m in range(8):
                        base = (avb + m // 4) * 512 + (m % 4) * 65
                        o("dve", lambda e, base=base: e.tensor_scalar(
                            SMALL[:, 41:42], ps[:, base + 64:base + 65], 1e-30, None, OP.max),
                          reads=pk(avb + m // 4), writes=["dmx"])
                        o("dve", lambda e: e.reciprocal(SMALL[:, 42:43], SMALL[:, 41:42]),
                          reads=["dmx"], writes=["rden"])
                        o("dve", lambda e, base=base, m=m, h=h: e.tensor_scalar(
                            a2[:, SLAB + m * 1024 + 64 * h:SLAB + m * 1024 + 64 * h + 64],
                            ps[:, base:base + 64], SMALL[:, 42:43], None, OP.mult),
                          reads=pk(avb + m // 4) + ["rden"], writes=["slab"])
                for m in range(8):
                    for g in range(2):
                        pb_ = g % 2
                        pview = psv(pb_, 512)
                        for j in range(4):
                            dc = g * 4 + j
                            o("pe", lambda e, m=m, dc=dc, j=j, pview=pview: e.matmul(
                                pview[:, j * 128:(j + 1) * 128],
                                a2[:, SLAB + m * 1024 + dc * 128:SLAB + m * 1024 + dc * 128 + 128],
                                IDB[:, :], start=(j == 0), stop=(j == 3), skip_group_check=True),
                              reads=["slab", "idb"], writes=pk(pb_))
                        def evt(e, m=m, g=g, pview=pview):
                            d = bass.AP(A2, DSL + (g * 4) * 1024 + m * 128,
                                        [[A2C, 128], [1024, 4], [1, 128]])
                            return e.tensor_copy(d, pview[:, :])
                        o("dve", evt, reads=pk(pb_), writes=["dsl"])
                for m in range(8):
                    t_t = qb // 128 + m
                    dma("sq", TMP[:, 1024:2048], xs[:, t_t * DM:(t_t + 1) * DM],
                        writes=["xl"], ch="xl")
                    yt = tmp[:, 2048:3072]
                    for n in range(2):
                        pb_ = 2 + n
                        po = psv(pb_)
                        for c in range(NCH):
                            o("pe", lambda e, m=m, n=n, c=c, po=po: e.matmul(
                                po, a2[:, DSL + c * 1024 + m * 128:DSL + c * 1024 + m * 128 + 128],
                                a2[:, WOO + c * 1024 + n * 512:WOO + c * 1024 + n * 512 + 512],
                                start=(c == 0), stop=(c == NCH - 1)),
                              reads=["dsl", "wo"], writes=pk(pb_))
                        acol = ACOL if is_short else ACOL + 2
                        o("dve", lambda e, po=po, n=n, acol=acol: e.scalar_tensor_tensor(
                            yt[:, n * 512:n * 512 + 512], po, BCOL[:, acol:acol + 1],
                            TMP[:, 1024 + n * 512:1024 + n * 512 + 512], OP.mult, OP.add),
                          reads=pk(pb_) + ["acol2", "xl"], writes=["yt"])
                    dma("sq", xs[:, t_t * DM:(t_t + 1) * DM], yt, reads=["yt"],
                        writes=["xl"], ch="xl")

        o("dve", lambda e: e.tensor_scalar(BCOL[:, ACOL + 2:ACOL + 3], BCOL[:, ACOL:ACOL + 1],
                                           -1.0, 1.0, OP.mult, OP.add),
          reads=["acol"], writes=["acol2"])
        o("dve", lambda e: e.tensor_copy(SMALL[:, 43:44], BCOL[:, ACOL:ACOL + 1]),
          reads=["acol"], writes=["acol2"])
        attn_phase(True)
        ctx.barrier()
        dma("sq", xab[:, KTO:KTO + 16384], ktl_s[:, :], reads=["ktspill"], writes=["kt"])
        dma("sq", a3[:, 0:16640], vl_s[:, :], reads=["vlspill"], writes=["vsb"])
        ctx.barrier()
        attn_phase(False)
        ctx.barrier()
        dma("sq", xa[:, :], xs[:, :], reads=["xl"], writes=XKEYS)
        ctx.barrier()

    def emit_moe_l1():
        emit_ln_to_xnT()
        ctx.barrier()
        W1O, W2O, R1O, WR1B, WR2B = 16384, 24576, 32768, 36864, 38912
        def bias_to_cols(src_flat, n, col0, key):
            m = n // 128
            dma("sq", TMP[0:m, 0:128], src_flat.rearrange("(r p) -> r p", p=128),
                writes=["brows"], ch="brows")
            pt = psv(7, 128)
            o("pe", lambda e, m=m: e.matmul(pt[:, 0:m], TMP[0:m, 0:128], IDF[0:m, 0:m],
                                            start=True, stop=True, skip_group_check=True),
              reads=["brows", "idf"], writes=pk(7))
            o("dve", lambda e, m=m, col0=col0: e.tensor_copy(BCOL[:, col0:col0 + m], pt[:, 0:m]),
              reads=pk(7), writes=[key])
        GTv = TMP[0:16, 1792:3840]
        bias_to_cols(W["L1_ffn_br1"][:], 256, 32, "br1c")
        dma("sq", SMALL[0:1, 56:64], W["L1_ffn_br2"][None, :], writes=["br2r"])
        dma("sq", BMAT[0:8, 0:1024], W["L1_ffn_be2"][:, :], writes=["bmat"])
        dma("sq", BMAT[8:9, 0:1024], W["L1_ffn_bs2"][None, :], writes=["bmat"])
        o("pool", lambda e: e.memset(bass.AP(GAT, 8, [[NT * 10, 128], [10, NT], [1, 1]]), 0.5),
          writes=["gat"])
        dma("pool", a2[:, WR1B:WR1B + 2048].rearrange("p (c m) -> p c m", c=8),
            w2d(W["L1_ffn_Wr1"], 0, DM, 0, 256), writes=["wr1b"])
        dma("pool", a2[:, WR2B:WR2B + 16].rearrange("p (c m) -> p c m", c=2),
            w2d(W["L1_ffn_Wr2"], 0, 256, 0, 8), writes=["wr2b"])
        for co in range(2):
            for nb in range(4):
                pb_ = (co * 4 + nb) % 2
                pq_ = psv(pb_)
                for c in range(NCH):
                    o("pe", lambda e, co=co, nb=nb, c=c, pq_=pq_: e.matmul(
                        pq_, a2[:, WR1B + c * 256 + co * 128:WR1B + c * 256 + co * 128 + 128],
                        a2[:, c * S + nb * 512:c * S + nb * 512 + 512],
                        start=(c == 0), stop=(c == NCH - 1)),
                      reads=["wr1b", "xnt"], writes=pk(pb_))
                o("act", lambda e, co=co, nb=nb, pq_=pq_: e.activation(
                    a2[:, R1O + co * 2048 + nb * 512:R1O + co * 2048 + nb * 512 + 512], pq_,
                    AF.Relu, bias=BCOL[:, 32 + co:32 + co + 1]),
                  reads=pk(pb_) + ["br1c"], writes=["r1t"])
        for t in range(NT):
            pg = psv(0, 8)
            for co in range(2):
                o("pe", lambda e, t=t, co=co, pg=pg: e.matmul(
                    pg, a2[:, R1O + co * 2048 + t * 128:R1O + co * 2048 + t * 128 + 128],
                    a2[:, WR2B + co * 8:WR2B + co * 8 + 8],
                    start=(co == 0), stop=False),
                  reads=["r1t", "wr2b"], writes=pk(0))
            o("pe", lambda e, t=t, pg=pg: e.matmul(
                pg, ONEF[0:1, 0:128], SMALL[0:1, 56:64], start=False, stop=True),
              reads=["br2r", "onef"], writes=pk(0))
            nm, sm, rs = SMALL[:, 8:9], SMALL[:, 9:10], SMALL[:, 10:11]
            m1, m2 = SMALL[:, 11:12], SMALL[:, 12:13]
            pu, pr = SMALL[:, 16:24], SMALL[:, 24:32]
            eq1, t2, tq = SMALL[:, 32:40], SMALL[:, 40:48], SMALL[:, 48:56]
            eq2, wa, wb_ = BCOL[:, 136:144], BCOL[:, 144:152], BCOL[:, 152:160]
            o("dve", lambda e, pg=pg: e.reduce_max(nm, pg, axis=AX, negate=True),
              reads=pk(0), writes=["g_nm"])
            o("act", lambda e, pg=pg: e.activation(pu, pg, AF.Exp, bias=nm),
              reads=pk(0) + ["g_nm"], writes=["g_pu"])
            o("dve", lambda e: e.reduce_sum(sm, pu, axis=AX), reads=["g_pu"], writes=["g_sm"])
            o("dve", lambda e: e.reciprocal(rs, sm), reads=["g_sm"], writes=["g_rs"])
            o("dve", lambda e: e.tensor_scalar(pr, pu, rs, None, OP.mult),
              reads=["g_pu", "g_rs"], writes=["g_pr"])
            o("dve", lambda e: e.reduce_max(m1, pr, axis=AX), reads=["g_pr"], writes=["g_m1"])
            o("dve", lambda e: e.tensor_scalar(eq1, pr, m1, None, OP.is_equal),
              reads=["g_pr", "g_m1"], writes=["g_eq1"])
            o("dve", lambda e: e.tensor_scalar(t2, eq1, 2.0, None, OP.mult),
              reads=["g_eq1"], writes=["g_t2"])
            o("dve", lambda e: e.tensor_sub(tq, pr, t2), reads=["g_pr", "g_t2"], writes=["g_tq"])
            o("dve", lambda e: e.reduce_max(m2, tq, axis=AX), reads=["g_tq"], writes=["g_m2"])
            o("dve", lambda e: e.tensor_scalar(eq2, pr, m2, None, OP.is_equal),
              reads=["g_pr", "g_m2"], writes=["g_eq2"])
            d21 = SMALL[:, 13:14]
            o("dve", lambda e: e.tensor_sub(d21, m1, m2), reads=["g_m1", "g_m2"], writes=["g_d21"])
            tg1, tg2 = SMALL[:, 14:15], SMALL[:, 15:16]
            o("act", lambda e: e.activation(tg1, d21, AF.Sigmoid, scale=0.5),
              reads=["g_d21"], writes=["g_tg1"])
            o("dve", lambda e: e.tensor_scalar(tg1, tg1, 0.5, None, OP.mult),
              reads=["g_tg1"], writes=["g_tg1"])
            o("dve", lambda e: e.tensor_scalar(tg2, tg1, -1.0, 0.5, OP.mult, OP.add),
              reads=["g_tg1"], writes=["g_tg2"])
            o("dve", lambda e: e.tensor_scalar(wa, eq1, tg1, None, OP.mult),
              reads=["g_eq1", "g_tg1"], writes=["g_wa"])
            o("dve", lambda e: e.tensor_scalar(wb_, eq2, tg2, None, OP.mult),
              reads=["g_eq2", "g_tg2"], writes=["g_wb"])
            wc_ = SMALL[:, 40:48]
            o("dve", lambda e: e.tensor_scalar(wc_, eq1, -1.0, 1.0, OP.mult, OP.add),
              reads=["g_eq1"], writes=["g_t2"])
            o("dve", lambda e: e.tensor_mul(wb_, wb_, wc_), reads=["g_wb", "g_t2"], writes=["g_wb"])
            o("dve", lambda e, t=t: e.tensor_add(GAT[:, t * 10:t * 10 + 8], wa, wb_),
              reads=["g_wa", "g_wb"], writes=["gat"])
        for t0 in range(0, NT, 4):
            pb_ = psv(7, 512)
            for j in range(4):
                t = t0 + j
                o("pe", lambda e, t=t, j=j, pb_=pb_: e.matmul(
                    pb_[0:9, j * 128:j * 128 + 128], GAT[:, t * 10:t * 10 + 9], IDF[:, 0:128],
                    start=(j == 0), stop=(j == 3), skip_group_check=True),
                  reads=["gat", "idf"], writes=pk(7))
            o("dve", lambda e, t0=t0, pb_=pb_: e.tensor_copy(GTv[0:9, t0 * 128:t0 * 128 + 512],
                                                             pb_[0:9, 0:512]),
              reads=pk(7), writes=["gt"])
        for t in range(NT):
            for n in range(2):
                pb_ = 2 + (t * 2 + n) % 2
                po = psv(pb_)
                o("pe", lambda e, t=t, n=n, po=po: e.matmul(
                    po, GTv[0:9, t * 128:t * 128 + 128], BMAT[0:9, n * 512:n * 512 + 512],
                    start=True, stop=True),
                  reads=["gt", "bmat"], writes=pk(pb_))
                o("dve", lambda e, po=po, t=t, n=n: e.tensor_tensor(
                    xa[:, t * DM + n * 512:t * DM + n * 512 + 512],
                    xa[:, t * DM + n * 512:t * DM + n * 512 + 512], po, OP.add),
                  reads=pk(pb_) + ["xa%d" % t], writes=["xa%d" % t])
        ctx.barrier()
        def l1_srcs(ei):
            if ei < 8:
                return (lambda qg: w3d(W["L1_ffn_We1"], ei, 0, DM, qg * 1024, (qg + 1) * 1024),
                        lambda qg: w3d(W["L1_ffn_We2"], ei, qg * 1024, (qg + 1) * 1024, 0, DM),
                        W["L1_ffn_be1"][ei:ei + 1, :].rearrange("a (r p) -> (a r) p", p=128))
            return (lambda qg: w2d(W["L1_ffn_Ws1"], 0, DM, qg * 1024, (qg + 1) * 1024),
                    lambda qg: w2d(W["L1_ffn_Ws2"], qg * 1024, (qg + 1) * 1024, 0, DM),
                    W["L1_ffn_bs1"][:].rearrange("(r p) -> r p", p=128))

        def l1_ldW1(ei, qg):
            dma("pool", a2[:, W1O:W1O + 8192].rearrange("p (c m) -> p c m", c=8),
                l1_srcs(ei)[0](qg), writes=["w1"])
        def l1_ldW2(ei, qg):
            dma("pool", a2[:, W2O:W2O + 8192].rearrange("p (c m) -> p c m", c=8),
                l1_srcs(ei)[1](qg), writes=["w2"])
        def l1_ldb(ei):
            dma("sq", TMP[0:32, 0:128], l1_srcs(ei)[2], writes=["brows"], ch="brows")
            pt = psv(7, 128)
            o("pe", lambda e: e.matmul(pt[:, 0:32], TMP[0:32, 0:128], IDF[0:32, 0:32],
                                       start=True, stop=True, skip_group_check=True),
              reads=["brows", "idf"], writes=pk(7))
            o("dve", lambda e: e.tensor_copy(BCOL[:, 34:66], pt[:, 0:32]),
              reads=pk(7), writes=["be1c"])

        l1_ldW1(0, 0); l1_ldW2(0, 0); l1_ldb(0)
        for nset in range(36):
            ei, qg = nset // 4, nset % 4
            for sb in range(4):
                s0 = sb * 512
                for h in range(8):
                    bs_ = h % 2
                    p1 = psv(0 + bs_)
                    for c in range(NCH):
                        o("pe", lambda e, c=c, h=h, s0=s0, p1=p1: e.matmul(
                            p1, a2[:, W1O + c * 1024 + h * 128:W1O + c * 1024 + h * 128 + 128],
                            a2[:, c * S + s0:c * S + s0 + 512],
                            start=(c == 0), stop=(c == NCH - 1)),
                          reads=["w1", "xnt"], writes=pk(0 + bs_))
                    o("act", lambda e, p1=p1, h=h, qg=qg: e.activation(
                        a3[:, h * 512:h * 512 + 512], p1, AF.Silu,
                        bias=BCOL[:, 34 + qg * 8 + h:34 + qg * 8 + h + 1]),
                      reads=pk(0 + bs_) + ["be1c"], writes=["hc%d" % h])
                if sb == 3 and nset < 35:
                    l1_ldW1((nset + 1) // 4, (nset + 1) % 4)
                    if (nset + 1) % 4 == 0:
                        l1_ldb((nset + 1) // 4)
                for m in range(4):
                    t_t = sb * 4 + m
                    for n in range(2):
                        bsel = 4 + (m * 2 + n) % 2
                        po = psv(bsel)
                        for h in range(8):
                            o("pe", lambda e, h=h, m=m, n=n, po=po: e.matmul(
                                po, a3[:, h * 512 + m * 128:h * 512 + m * 128 + 128],
                                a2[:, W2O + h * 1024 + n * 512:W2O + h * 1024 + n * 512 + 512],
                                start=(h == 0), stop=(h == 7)),
                              reads=["hc%d" % h, "w2"], writes=pk(bsel))
                        o("dve", lambda e, po=po, t_t=t_t, n=n, ei=ei: e.scalar_tensor_tensor(
                            xa[:, t_t * DM + n * 512:t_t * DM + n * 512 + 512], po,
                            GAT[:, t_t * 10 + ei:t_t * 10 + ei + 1],
                            xa[:, t_t * DM + n * 512:t_t * DM + n * 512 + 512],
                            OP.mult, OP.add),
                          reads=pk(bsel) + ["gat", "xa%d" % t_t], writes=["xa%d" % t_t])
                if sb == 3 and nset < 35:
                    l1_ldW2((nset + 1) // 4, (nset + 1) % 4)

    def tk(a, b):
        return ["T%d" % i for i in range(a // 256, (b + 255) // 256)]

    def emit_attn_l1():
        emit_ln_to_xnT()
        ctx.barrier()
        QPT, WB, FTO, WGB0, WGB1 = 16384, 32768, 16384, 38912, 40960
        KVO = 32768
        dma("sq", xs[:, :], xa[:, :], reads=XKEYS, writes=["xospill"])
        ctx.barrier()

        def cols_from(src128, m, col0, key, ragged_tail=0):
            if ragged_tail:
                o("pool", lambda e, m=m: e.memset(TMP[0:m, 0:128], 0.0), writes=tk(0, 128))
                full = (m - 1) * 128
                dma("sq", TMP[0:m - 1, 0:128], src128[0:full].rearrange("(r p) -> r p", p=128),
                    writes=tk(0, 128), ch="brows")
                dma("sq", TMP[m - 1:m, 0:ragged_tail], src128[None, full:full + ragged_tail],
                    writes=tk(0, 128), ch="brows")
            else:
                dma("sq", TMP[0:m, 0:128], src128.rearrange("(r p) -> r p", p=128),
                    writes=tk(0, 128), ch="brows")
            pt = psv(7, 128)
            o("pe", lambda e, m=m: e.matmul(pt[:, 0:m], TMP[0:m, 0:128], IDF[0:m, 0:m],
                                            start=True, stop=True, skip_group_check=True),
              reads=tk(0, 128) + ["idf"], writes=pk(7))
            o("dve", lambda e, m=m, col0=col0: e.tensor_copy(BCOL[:, col0:col0 + m], pt[:, 0:m]),
              reads=pk(7), writes=[key])

        cols_from(W["L1_attn_bqkv"][0:DM], 8, 0, "bqc")
        o("dve", lambda e: e.tensor_scalar(BCOL[:, 8:16], BCOL[:, 0:8], -1.0, None, OP.mult),
          reads=["bqc"], writes=["nbqc"])
        dma("pool", a2[:, WB:WB + 8192].rearrange("p (c m) -> p c m", c=8),
            w2d(W["L1_attn_Wqkv"], 0, DM, 0, DM), writes=["wqkv"])
        scr0, scr1 = TMP[:, 0:512], TMP[:, 512:1024]
        K0, K1 = tk(0, 512), tk(512, 1024)
        for co in range(8):
            for nb in range(4):
                pb_ = (co * 4 + nb) % 2
                pq_ = psv(pb_)
                for c in range(NCH):
                    o("pe", lambda e, co=co, c=c, nb=nb, pq_=pq_: e.matmul(
                        pq_, a2[:, WB + c * 1024 + co * 128:WB + c * 1024 + co * 128 + 128],
                        a2[:, c * S + nb * 512:c * S + nb * 512 + 512],
                        start=(c == 0), stop=(c == NCH - 1)),
                      reads=["wqkv", "xnt"], writes=pk(pb_))
                o("act", lambda e, pq_=pq_, co=co: e.activation(
                    scr0, pq_, AF.Relu, bias=BCOL[:, co:co + 1]),
                  reads=pk(pb_) + ["bqc"], writes=K0)
                o("act", lambda e, pq_=pq_, co=co: e.activation(
                    scr1, pq_, AF.Relu, bias=BCOL[:, 8 + co:8 + co + 1], scale=-1.0),
                  reads=pk(pb_) + ["nbqc"], writes=K1)
                o("act", lambda e: e.activation(scr1, scr1, AF.Exp, scale=-1.0),
                  reads=K1, writes=K1)
                o("dve", lambda e, co=co, nb=nb: e.tensor_tensor(
                    a2[:, QPT + co * 2048 + nb * 512:QPT + co * 2048 + nb * 512 + 512],
                    scr0, scr1, OP.add),
                  reads=K0 + K1, writes=["qpt"])
        if L1CUT < 1:
            return
        dma("sq", TMP[0:1, 1024:2048], W["L1_attn_bqkv"][None, DM:2 * DM],
            writes=tk(1024, 2048), ch="brows")
        dma("sq", TMP[0:1, 2560:3584], W["L1_attn_bqkv"][None, 2 * DM:3 * DM],
            writes=tk(2560, 3584), ch="brows")
        dma("pool", a2[:, WB:WB + 8192].rearrange("p (c m) -> p c m", c=8),
            w2d(W["L1_attn_Wqkv"], 0, DM, DM, 2 * DM), writes=["wqkv"])
        kpkeys = ["kp"] + ["xa%d" % t for t in range(8)]
        for t in range(NT):
            for nb in range(2):
                pb_ = 2 + (t * 2 + nb) % 2
                pq_ = psv(pb_)
                for c in range(NCH):
                    o("pe", lambda e, t=t, nb=nb, c=c, pq_=pq_: e.matmul(
                        pq_, a2[:, c * S + t * 128:c * S + t * 128 + 128],
                        a2[:, WB + c * 1024 + nb * 512:WB + c * 1024 + nb * 512 + 512],
                        start=(c == 0), stop=False),
                      reads=["wqkv", "xnt"], writes=pk(pb_))
                o("pe", lambda e, nb=nb, pq_=pq_: e.matmul(
                    pq_, ONEF[0:1, 0:128], TMP[0:1, 1024 + nb * 512:1024 + nb * 512 + 512],
                    start=False, stop=True),
                  reads=tk(1024, 2048) + ["onef"], writes=pk(pb_))
                o("act", lambda e, pq_=pq_: e.activation(scr0, pq_, AF.Relu),
                  reads=pk(pb_), writes=K0)
                o("act", lambda e, pq_=pq_: e.activation(scr1, pq_, AF.Relu, scale=-1.0),
                  reads=pk(pb_), writes=K1)
                o("act", lambda e: e.activation(scr1, scr1, AF.Exp, scale=-1.0),
                  reads=K1, writes=K1)
                o("dve", lambda e, t=t, nb=nb: e.tensor_tensor(
                    xab[:, t * 1024 + nb * 512:t * 1024 + nb * 512 + 512],
                    scr0, scr1, OP.add),
                  reads=K0 + K1, writes=kpkeys)
        if L1CUT < 2:
            return
        dma("pool", a2[:, WB:WB + 8192].rearrange("p (c m) -> p c m", c=8),
            w2d(W["L1_attn_Wqkv"], 0, DM, 2 * DM, 3 * DM), writes=["wqkv"])
        o("pool", lambda e: e.memset(bass.AP(A3, 1024, [[NT * 1040, 128], [1025, NT], [1, 1]]), 1.0),
          writes=["vsb"])
        for t in range(NT):
            for nb in range(2):
                pb_ = 2 + (t * 2 + nb) % 2
                pq_ = psv(pb_)
                for c in range(NCH):
                    o("pe", lambda e, t=t, nb=nb, c=c, pq_=pq_: e.matmul(
                        pq_, a2[:, c * S + t * 128:c * S + t * 128 + 128],
                        a2[:, WB + c * 1024 + nb * 512:WB + c * 1024 + nb * 512 + 512],
                        start=(c == 0), stop=False),
                      reads=["wqkv", "xnt"], writes=pk(pb_))
                o("pe", lambda e, nb=nb, pq_=pq_: e.matmul(
                    pq_, ONEF[0:1, 0:128], TMP[0:1, 2560 + nb * 512:2560 + nb * 512 + 512],
                    start=False, stop=True),
                  reads=tk(2560, 3584) + ["onef"], writes=pk(pb_)),
                o("dve", lambda e, t=t, nb=nb, pq_=pq_: e.tensor_copy(
                    a3[:, t * 1025 + nb * 512:t * 1025 + nb * 512 + 512], pq_),
                  reads=pk(pb_), writes=["vsb"])
        ctx.barrier()
        if L1CUT < 3:
            return
        for dc in range(8):
            for nb in range(2):
                pb_ = (dc % 2) * 2 + nb
                pq_ = psv(pb_)
                for t in range(NT):
                    o("pe", lambda e, dc=dc, nb=nb, t=t, pq_=pq_: e.matmul(
                        pq_, xab[:, t * 1024 + dc * 128:t * 1024 + dc * 128 + 128],
                        a3[:, t * 1025 + nb * 512:t * 1025 + nb * 512 + 512],
                        start=(t == 0), stop=(t == NT - 1)),
                      reads=["kp", "vsb"], writes=pk(pb_))
                o("dve", lambda e, dc=dc, nb=nb, pq_=pq_: e.tensor_copy(
                    a2[:, KVO + dc * 1024 + nb * 512:KVO + dc * 1024 + nb * 512 + 512], pq_),
                  reads=pk(pb_), writes=["kv"])
            for t in range(NT):
                o("pe", lambda e, dc=dc, t=t: e.matmul(
                    ps[:, 4 * 512 + dc:4 * 512 + dc + 1],
                    xab[:, t * 1024 + dc * 128:t * 1024 + dc * 128 + 128],
                    a3[:, t * 1025 + 1024:t * 1025 + 1025],
                    start=(t == 0), stop=(t == NT - 1), skip_group_check=True),
                  reads=["kp", "vsb"], writes=pk(4))
        o("dve", lambda e: e.tensor_copy(a3[:, 16400:16408], ps[:, 4 * 512:4 * 512 + 8]),
          reads=pk(4), writes=["ksum"])
        ctx.barrier()
        if L1CUT < 4:
            return
        for t in range(NT):
            dma("sq", TMP[:, 1024:2048], xs[:, t * DM:(t + 1) * DM],
                writes=tk(1024, 2048), ch="xl")
            for nb in range(2):
                pb_ = (t % 2) * 2 + nb
                pq_ = psv(pb_)
                for c in range(NCH):
                    o("pe", lambda e, t=t, nb=nb, c=c, pq_=pq_: e.matmul(
                        pq_, a2[:, QPT + c * 2048 + t * 128:QPT + c * 2048 + t * 128 + 128],
                        a2[:, KVO + c * 1024 + nb * 512:KVO + c * 1024 + nb * 512 + 512],
                        start=(c == 0), stop=(c == NCH - 1)),
                      reads=["qpt", "kv"], writes=pk(pb_))
            for c in range(NCH):
                o("pe", lambda e, t=t, c=c: e.matmul(
                    ps[:, 5 * 512 + t % 2:5 * 512 + t % 2 + 1],
                    a2[:, QPT + c * 2048 + t * 128:QPT + c * 2048 + t * 128 + 128],
                    a3[:, 16400 + c:16400 + c + 1],
                    start=(c == 0), stop=(c == NCH - 1), skip_group_check=True),
                  reads=["qpt", "ksum"], writes=pk(5))
            o("dve", lambda e, t=t: e.tensor_scalar(
                SMALL[:, 44:45], ps[:, 5 * 512 + t % 2:5 * 512 + t % 2 + 1],
                1e-6, None, OP.add),
              reads=pk(5), writes=["deneps"])
            o("dve", lambda e: e.reciprocal(SMALL[:, 45:46], SMALL[:, 44:45]),
              reads=["deneps"], writes=["rden1"])
            for nb in range(2):
                pb_ = (t % 2) * 2 + nb
                o("dve", lambda e, t=t, nb=nb, pb_=pb_: e.scalar_tensor_tensor(
                    xa[:, t * DM + nb * 512:t * DM + nb * 512 + 512], psv(pb_),
                    SMALL[:, 45:46], TMP[:, 1024 + nb * 512:1024 + nb * 512 + 512],
                    OP.mult, OP.add),
                  reads=pk(pb_) + ["rden1", "kp", "kv"] + tk(1024, 2048), writes=["xa%d" % t])
        ctx.barrier()
        if L1CUT < 5:
            return
        grep = a2[:, FTO:FTO + 2048].bitcast(F32)
        brep = a2[:, FTO + 2048:FTO + 4096].bitcast(F32)

        def load_rep():
            for n in range(2):
                pr_ = psv(6 + n)
                o("pe", lambda e, n=n, pr_=pr_: e.matmul(
                    pr_, ONEF[0:1, 0:128], TMP[0:1, n * 512:n * 512 + 512],
                    start=True, stop=True),
                  reads=tk(0, 1024) + ["onef"], writes=pk(6 + n))
                o("dve", lambda e, n=n, pr_=pr_: e.tensor_copy(
                    grep[:, n * 512:n * 512 + 512], pr_), reads=pk(6 + n), writes=["grep"])
            for n in range(2):
                pr_ = psv(6 + n)
                o("pe", lambda e, n=n, pr_=pr_: e.matmul(
                    pr_, ONEF[0:1, 0:128], TMP[0:1, 1024 + n * 512:1024 + n * 512 + 512],
                    start=True, stop=True),
                  reads=tk(1024, 2048) + ["onef"], writes=pk(6 + n))
                o("dve", lambda e, n=n, pr_=pr_: e.tensor_copy(
                    brep[:, n * 512:n * 512 + 512], pr_), reads=pk(6 + n), writes=["brep"])

        def ln_stats(t, kx, joff):
            sumc, sqc = SMALL[:, 0:1], SMALL[:, 1:2]
            junkD = tmp[:, joff:joff + 512].bitcast(BF16)
            junkA = tmp[:, joff + 512:joff + 1024].bitcast(BF16)
            xt = xa[:, t * DM:(t + 1) * DM]
            o("dve", lambda e, xt=xt, junkD=junkD: e.tensor_scalar(
                junkD, xt, 1.0, 0.0, OP.mult, OP.add, accum_out=sumc),
              reads=[kx], writes=["ln_sum"] + tk(joff, joff + 512))
            o("act", lambda e, xt=xt, junkA=junkA: e.activation(
                junkA, xt, AF.Square, accum_out=sqc),
              reads=[kx], writes=["ln_sq"] + tk(joff + 512, joff + 1024))
            mu, t2, mu2, ve = SMALL[:, 2:3], SMALL[:, 3:4], SMALL[:, 4:5], SMALL[:, 5:6]
            sd, rstd = SMALL[:, 6:7], SMALL[:, 7:8]
            o("dve", lambda e: e.tensor_scalar(mu, sumc, 1.0 / DM, None, OP.mult),
              reads=["ln_sum"], writes=["ln_mu"])
            o("dve", lambda e: e.tensor_scalar(t2, sqc, 1.0 / DM, 1e-5, OP.mult, OP.add),
              reads=["ln_sq"], writes=["ln_t2"])
            o("dve", lambda e: e.tensor_mul(mu2, mu, mu), reads=["ln_mu"], writes=["ln_mu2"])
            o("dve", lambda e: e.tensor_sub(ve, t2, mu2), reads=["ln_t2", "ln_mu2"], writes=["ln_ve"])
            o("act", lambda e: e.activation(sd, ve, AF.Sqrt), reads=["ln_ve"], writes=["ln_sd"])
            o("dve", lambda e: e.reciprocal(rstd, sd), reads=["ln_sd"], writes=["ln_rstd"])
            return mu, rstd

        dma("sq", TMP[0:1, 0:1024], W["L1_attn_g1"][None, :], writes=tk(0, 1024), ch="brows")
        dma("sq", TMP[0:1, 1024:2048], W["L1_attn_b1"][None, :], writes=tk(1024, 2048), ch="brows")
        load_rep()
        for t in range(NT):
            kx = "xa%d" % t
            mu, rstd = ln_stats(t, kx, 1024)
            tt = TMP[:, 2048:3072]
            o("dve", lambda e, t=t, mu=mu, rstd=rstd, tt=tt: e.tensor_scalar(
                tt, xa[:, t * DM:(t + 1) * DM], mu, rstd, OP.subtract, OP.mult),
              reads=[kx, "ln_mu", "ln_rstd"], writes=tk(2048, 3072))
            o("dve", lambda e, tt=tt: e.tensor_mul(tt, tt, grep),
              reads=tk(2048, 3072) + ["grep"], writes=tk(2048, 3072))
            o("dve", lambda e, t=t, tt=tt: e.tensor_add(xa[:, t * DM:(t + 1) * DM], tt, brep),
              reads=tk(2048, 3072) + ["brep"], writes=[kx])
            xnb = tmp[:, 0:512].bitcast(BF16)
            o("act", lambda e, t=t, xnb=xnb: e.activation(xnb, xa[:, t * DM:(t + 1) * DM], AF.Copy),
              reads=[kx], writes=tk(0, 1024))
            for half in range(2):
                pb_ = 6 + half
                pview = psv(pb_, 512)
                for j in range(4):
                    c = half * 4 + j
                    o("pe", lambda e, c=c, j=j, pview=pview, xnb=xnb: e.matmul(
                        pview[:, j * 128:(j + 1) * 128], xnb[:, c * 128:(c + 1) * 128],
                        IDB[:, :], start=(j == 0), stop=(j == 3), skip_group_check=True),
                      reads=tk(0, 1024) + ["idb"], writes=pk(pb_))
                def evac(e, half=half, t=t, pview=pview):
                    d = bass.AP(A2, (half * 4) * S + t * 128, [[A2C, 128], [S, 4], [1, 128]])
                    return e.tensor_copy(d, pview[:, :])
                o("dve", evac, reads=pk(pb_), writes=["x2t"])
        ctx.barrier()
        if L1CUT < 6:
            return
        NHID = 22

        def hsz(h):
            return 42 if h == NHID - 1 else 128

        def rag_cols(src, col0, key):
            cols_from(src, 22, col0, key, ragged_tail=42)

        rag_cols(W["L1_attn_bg"][:], 96, "bgc")
        rag_cols(W["L1_attn_bu"][:], 118, "buc")
        dma("sq", TMP[0:1, 2560:3584], W["L1_attn_bd"][None, :], writes=tk(2560, 3584), ch="brows")
        sb16 = tmp[:, 0:256].bitcast(BF16)
        for sh in (0, 1024):
            for h in range(NHID):
                hs = hsz(h)
                if hs == 128:
                    gsrc = w2d(W["L1_attn_Wg"], 0, DM, h * 128, h * 128 + hs)
                    usrc = w2d(W["L1_attn_Wu"], 0, DM, h * 128, h * 128 + hs)
                else:
                    gsrc = W["L1_attn_Wg"][:, h * 128:h * 128 + hs].rearrange(
                        "(c p) m -> p c m", p=128)
                    usrc = W["L1_attn_Wu"][:, h * 128:h * 128 + hs].rearrange(
                        "(c p) m -> p c m", p=128)
                GS = (WGB0 if h % 2 == 0 else WGB0 + 1024)
                US = (WGB1 if h % 2 == 0 else WGB1 + 1024)
                dma("pool", a2[:, GS:GS + 8 * hs].rearrange("p (c m) -> p c m", c=8),
                    gsrc, writes=["slot0%d" % (h % 2)])
                dma("pool", a2[:, US:US + 8 * hs].rearrange("p (c m) -> p c m", c=8),
                    usrc, writes=["slot1%d" % (h % 2)])
                for nb in range(2):
                    pg_ = psv((h % 2) * 2)
                    pu_ = psv(4 + (h % 2) * 2)
                    for c in range(NCH):
                        o("pe", lambda e, hs=hs, c=c, nb=nb, sh=sh, pg_=pg_, GS=GS: e.matmul(
                            pg_[0:hs, 0:512], a2[:, GS + c * hs:GS + c * hs + hs],
                            a2[:, c * S + sh + nb * 512:c * S + sh + nb * 512 + 512],
                            start=(c == 0), stop=(c == NCH - 1), skip_group_check=True),
                          reads=["slot0%d" % (h % 2), "x2t"], writes=pk((h % 2) * 2))
                    for c in range(NCH):
                        o("pe", lambda e, hs=hs, c=c, nb=nb, sh=sh, pu_=pu_, US=US: e.matmul(
                            pu_[0:hs, 0:512], a2[:, US + c * hs:US + c * hs + hs],
                            a2[:, c * S + sh + nb * 512:c * S + sh + nb * 512 + 512],
                            start=(c == 0), stop=(c == NCH - 1), skip_group_check=True),
                          reads=["slot1%d" % (h % 2), "x2t"], writes=pk(4 + (h % 2) * 2))
                    o("act", lambda e, hs=hs, h=h, pg_=pg_: e.activation(
                        sb16[0:hs, :], pg_[0:hs, 0:512], AF.Silu,
                        bias=BCOL[0:hs, 96 + h:96 + h + 1]),
                      reads=pk((h % 2) * 2) + ["bgc"], writes=tk(0, 256))
                    o("dve", lambda e, hs=hs, h=h, nb=nb, pu_=pu_: e.scalar_tensor_tensor(
                        a2[0:hs, FTO + h * 1024 + nb * 512:FTO + h * 1024 + nb * 512 + 512],
                        pu_[0:hs, 0:512], BCOL[0:hs, 118 + h:118 + h + 1], sb16[0:hs, :],
                        OP.add, OP.mult),
                      reads=pk(4 + (h % 2) * 2) + ["buc"] + tk(0, 256), writes=["ft"])
            for q4 in range(2):
                s0 = sh + q4 * 512
                for h in range(NHID):
                    hs = hsz(h)
                    wsl = WGB0 if h % 2 == 0 else WGB1
                    if hs == 128:
                        dsrc = w2d(W["L1_attn_Wd"], h * 128, h * 128 + hs, 0, DM)
                    else:
                        dsrc = W["L1_attn_Wd"][h * 128:h * 128 + hs, :].rearrange(
                            "(c p) m -> p c m", p=42)
                    dma("pool", a2[0:hs, wsl:wsl + 1024].rearrange("p (c m) -> p c m", c=1),
                        dsrc, writes=["slot00", "slot01", "slot10", "slot11"])
                    for m in range(4):
                        for n in range(2):
                            pb_ = m * 2 + n
                            o("pe", lambda e, h=h, hs=hs, m=m, n=n, q4=q4, wsl=wsl, pb_=pb_: e.matmul(
                                psv(pb_),
                                a2[0:hs, FTO + h * 1024 + q4 * 512 + m * 128:
                                   FTO + h * 1024 + q4 * 512 + m * 128 + 128],
                                a2[0:hs, wsl + n * 512:wsl + n * 512 + 512],
                                start=(h == 0), stop=False, skip_group_check=True),
                              reads=["ft", "slot00" if h % 2 == 0 else "slot10"], writes=pk(pb_))
                for m in range(4):
                    for n in range(2):
                        pb_ = m * 2 + n
                        o("pe", lambda e, n=n, pb_=pb_: e.matmul(
                            psv(pb_), ONEF[0:1, 0:128],
                            TMP[0:1, 2560 + n * 512:2560 + n * 512 + 512],
                            start=False, stop=True, skip_group_check=True),
                          reads=tk(2560, 3584) + ["onef"], writes=pk(pb_))
                for m in range(4):
                    t_t = s0 // 128 + m
                    for n in range(2):
                        pb_ = m * 2 + n
                        o("dve", lambda e, t_t=t_t, n=n, pb_=pb_: e.tensor_tensor(
                            xa[:, t_t * DM + n * 512:t_t * DM + n * 512 + 512],
                            xa[:, t_t * DM + n * 512:t_t * DM + n * 512 + 512],
                            psv(pb_), OP.add),
                          reads=pk(pb_) + ["xa%d" % t_t], writes=["xa%d" % t_t])
        ctx.barrier()
        if L1CUT < 8:
            return
        dma("sq", TMP[0:1, 0:1024], W["L1_attn_g2"][None, :], writes=tk(0, 1024), ch="brows")
        dma("sq", TMP[0:1, 1024:2048], W["L1_attn_b2"][None, :], writes=tk(1024, 2048), ch="brows")
        load_rep()
        for t in range(NT):
            kx = "xa%d" % t
            mu, rstd = ln_stats(t, kx, 0)
            tt = TMP[:, 2048:3072]
            dma("sq", TMP[:, 1024:2048], xs[:, t * DM:(t + 1) * DM],
                writes=tk(1024, 2048), ch="xl")
            o("dve", lambda e, t=t, mu=mu, rstd=rstd, tt=tt: e.tensor_scalar(
                tt, xa[:, t * DM:(t + 1) * DM], mu, rstd, OP.subtract, OP.mult),
              reads=[kx, "ln_mu", "ln_rstd"], writes=tk(2048, 3072))
            o("dve", lambda e, tt=tt: e.tensor_mul(tt, tt, grep),
              reads=tk(2048, 3072) + ["grep"], writes=tk(2048, 3072))
            o("dve", lambda e, tt=tt: e.tensor_add(tt, tt, brep),
              reads=tk(2048, 3072) + ["brep"], writes=tk(2048, 3072))
            o("dve", lambda e, t=t, tt=tt: e.tensor_add(
                xa[:, t * DM:(t + 1) * DM], tt, TMP[:, 1024:2048]),
              reads=tk(2048, 3072) + tk(1024, 2048), writes=[kx])
        ctx.barrier()

    def emit_store():
        ctx.barrier()
        dma("sq", y_out.rearrange("(t p) d -> p t d", p=128),
            xa.rearrange("p (t d) -> p t d", t=NT), reads=XKEYS, writes=["yout"])
        o("pool", lambda e: e.memset(SMALL[0:1, 63:64], 0.0), reads=["yout"], writes=["fin"])

    setup()
    ctx.barrier()
    subs = [emit_moe_l0, emit_attn_l0, emit_moe_l0, emit_moe_l1, emit_attn_l1, emit_moe_l1]
    for i in range(min(nsub, len(subs))):
        subs[i]()
        ctx.barrier()
    emit_store()
    keys = set()
    for e in Ctx.ENGS:
        keys.add(e)
        for c in ctx.cmds[e]:
            keys.add(c.skey)
    for key in sorted(keys):
        cm = nc.semaphore("sem_" + key); sems[key] = cm.__enter__(); cms.append(cm)
    ctx.finalize(sems)
    for cm in reversed(cms):
        cm.__exit__(None, None, None)
    return nc



def make_in_maps(inputs):
    x = np.ascontiguousarray(np.asarray(inputs["L0_attn_x"], dtype=np.float32))
    base = {}
    for k, v in inputs.items():
        if k == "L0_attn_x":
            continue
        a = np.ascontiguousarray(np.asarray(v, dtype=np.float32))
        if a.ndim == 0:
            a = a.reshape(1, 1)
        base[k] = a
    return [dict(base, x=np.ascontiguousarray(x[c])) for c in range(8)]


def kernel(**inputs):
    nc = build_program(N_SUB)
    in_maps = make_in_maps(inputs)
    res = run_bass_kernel_spmd(nc, in_maps, core_ids=list(range(8)))
    return np.stack([r["y"] for r in res.results], axis=0).astype(np.float32)
